# version 07
# baseline (speedup 1.0000x reference)
import numpy as np
import concourse.bass as bass
import concourse.mybir as mybir
from concourse.bass_utils import run_bass_kernel_spmd
from concourse.alu_op_type import AluOpType as OP

F32 = mybir.dt.float32
AF = mybir.ActivationFunctionType
AX = mybir.AxisListType

D = 1024
NH = 16
HD = 64
WIN = 256
EXT = 128
NB = 4
SEQ = 4096
TPC = 2048
HALO = TPC + 2 * EXT
NCH = D // 128
NE = 16
NEX = 17
TOPK = 8
RSCALE = 2.826
EPS = 1e-5
NEGBIG = -1e9
SCALE = HD ** -0.5

NQTR = 4
QTR = 512
NHF = 2
HF = 1024

def wq_map():
    m = {}
    off = 0
    for quad in range(2):
        m[("v", quad)] = off
        off += 8 * 512
        for pq in range(4):
            m[("qk", quad * 4 + pq)] = off
            off += 16 * 128
    return m, off

WQ_MAP, WQ_COLS = wq_map()
WFFN_E_COLS = 16 * 1024 + 8 * 1024
WFFN_COLS = NEX * WFFN_E_COLS

A_X0Q = 0;        A_X0Q_N = 8 * 768
A_WV = 6144;      A_WV_N = 8 * 512
A_WQK = 10240;    A_WQK_N = 2 * 2048
A_WP = 14336;     A_WP_N = 2 * 1024
A_Q = 16384;      A_Q_N = 512
A_K = 16896;      A_K_N = 768
A_V = 17664;      A_V_N = 6 * 512
A_P = 20736;      A_P_N = 4 * 256
A_ATT = 21760;    A_ATT_N = 8 * 512
A_SQ = 25856;     A_SQ_N = 512
A_LNR = 26368;    A_LNR_N = 512
A_NEG = 26880;    A_NEG_N = 12
A_LINV = 26892;   A_LINV_N = 256
A_MSK = 27148;    A_MSK_N = 256
ATT_END = A_MSK + A_MSK_N

F_Y = 0;          F_Y_N = 8 * 1024
F_H = 8192;       F_H_N = 8 * 1024
F_WGU = 16384;    F_WGU_N = 2048
F_WGU16 = 18432;  F_WGU16_N = 4096
F_WD = 22528;     F_WD_N = 1024
F_WD16 = 23552;   F_WD16_N = 2048
F_CB = 25600;     F_CB_N = 2 * 512
F_S = 26624;      F_S_N = 512
F_T = 27136;      F_T_N = 512
F_SQ = 27648;     F_SQ_N = 512
F_LNR = 28160;    F_LNR_N = 512
FFN_END = F_LNR + F_LNR_N

R_MS = 28672
R_VAR = 29184
A_XF = 29696
A_P2 = 30208
ARENA_COLS = 31232

ENGINES = ("sync", "tensor", "vector", "scalar", "gpsimd")

DEBUG_DUMP = False
PHASES = "all"


class Prog:

    def __init__(self):
        self.ops = {e: [] for e in ENGINES}
        self.counts = {}
        self.guards = {}

    def sem(self, name):
        self.counts[name] = 0

    def op(self, eng, fn, inc=None, inc_by=1):
        val = None
        if inc is not None:
            self.counts[inc] += inc_by
            val = self.counts[inc]
        self.ops[eng].append(("op", fn, inc, inc_by))
        return val

    def wait(self, eng, sem, val):
        assert val <= self.counts[sem], (sem, val, self.counts[sem])
        if val > 0:
            self.ops[eng].append(("wait", sem, val))

    def consumed(self, key, sem, val):
        g = self.guards.setdefault(key, {})
        g[sem] = max(g.get(sem, 0), val)

    def acquire(self, eng, key):
        g = self.guards.pop(key, None)
        if g:
            for sem, val in g.items():
                self.wait(eng, sem, val)

    def emit(self, eng, proxy, sems):
        for item in self.ops[eng]:
            if item[0] == "wait":
                _, sem, val = item
                proxy.wait_ge(sems[sem], val)
            else:
                _, fn, inc, inc_by = item
                instr = fn(proxy, sems)
                if inc is not None:
                    instr.then_inc(sems[inc], inc_by)


def build_program():
    nc = bass.Bass("TRN2", target_bir_lowering=False, debug=False,
                   detect_race_conditions=False, dynamic_dma_scratch_size=4096)

    x0 = nc.dram_tensor("x0", [128, NCH * HALO], F32, kind="ExternalInput").ap()
    mskD = nc.dram_tensor("msk", [1, 256], F32, kind="ExternalInput").ap()
    cstD = nc.dram_tensor("cst", [128, 2432], F32, kind="ExternalInput").ap()
    wqD = nc.dram_tensor("wq", [128, WQ_COLS], F32, kind="ExternalInput").ap()
    wpD = nc.dram_tensor("wp", [128, 8192], F32, kind="ExternalInput").ap()
    gwD = nc.dram_tensor("gw", [128, 128], F32, kind="ExternalInput").ap()
    ebD = nc.dram_tensor("eb", [128, 16], F32, kind="ExternalInput").ap()
    bqkD = nc.dram_tensor("bqk", [128, 16], F32, kind="ExternalInput").ap()
    bvD = nc.dram_tensor("bv", [128, 1024], F32, kind="ExternalInput").ap()
    bpD = nc.dram_tensor("bp", [128, 8], F32, kind="ExternalInput").ap()
    wffnD = nc.dram_tensor("wffn", [128, WFFN_COLS], F32, kind="ExternalInput").ap()
    outD = nc.dram_tensor("y", [128, NCH * TPC], F32, kind="ExternalOutput").ap()
    if DEBUG_DUMP:
        dbgD = nc.dram_tensor("dbg", [128, NCH * TPC], F32, kind="ExternalOutput").ap()

    BF16 = mybir.dt.bfloat16
    arena = nc.alloc_sbuf_tensor("arena", [128, ARENA_COLS], F32).ap()
    x1h = nc.alloc_sbuf_tensor("x1h", [128, NCH, TPC], BF16).ap()
    x1l = nc.alloc_sbuf_tensor("x1l", [128, NCH, TPC], BF16).ap()
    gwh = nc.alloc_sbuf_tensor("gwh", [128, 128], BF16).ap()
    gwl = nc.alloc_sbuf_tensor("gwl", [128, 128], BF16).ap()
    cst = nc.alloc_sbuf_tensor("cst_sb", [128, 2432], F32).ap()
    gw = nc.alloc_sbuf_tensor("gw_sb", [128, 128], F32).ap()
    eb = nc.alloc_sbuf_tensor("eb_sb", [128, 16], F32).ap()
    bqk = nc.alloc_sbuf_tensor("bqk_sb", [128, 16], F32).ap()
    bv = nc.alloc_sbuf_tensor("bv_sb", [128, 1024], F32).ap()
    bp = nc.alloc_sbuf_tensor("bp_sb", [128, 8], F32).ap()
    comb = nc.alloc_sbuf_tensor("comb_sb", [16, TPC], F32).ap()
    r_sc = nc.alloc_sbuf_tensor("r_sc", [128, 16], F32).ap()
    r_sel = nc.alloc_sbuf_tensor("r_sel", [128, 16], F32).ap()
    r_acc = nc.alloc_sbuf_tensor("r_acc", [128, 16], F32).ap()
    r_thr = nc.alloc_sbuf_tensor("r_thr", [128, 1], F32).ap()
    r_mk = nc.alloc_sbuf_tensor("r_mk", [128, 16], F32).ap()
    r_w = nc.alloc_sbuf_tensor("r_w", [128, 16], F32).ap()
    r_s = nc.alloc_sbuf_tensor("r_s", [128, 2], F32).ap()

    ones_col = cst[:, 0:64]
    ident = cst[:, 256:384]
    sel_c = cst[:, 384:2432]

    x0q = arena[:, A_X0Q:A_X0Q + A_X0Q_N].rearrange("p (c t) -> p c t", c=NCH)
    zv = x0q[:, :, 0:512]
    wv_b = arena[:, A_WV:A_WV + A_WV_N].rearrange("p (c n) -> p c n", c=8)
    wqk_b = arena[:, A_WQK:A_WQK + A_WQK_N].rearrange("p (b s n) -> p b s n", b=2, s=16)
    wp_b = arena[:, A_WP:A_WP + A_WP_N].rearrange("p (b s n) -> p b s n", b=2, s=8)
    q_p = arena[:, A_Q:A_Q + A_Q_N]
    k_p = arena[:, A_K:A_K + A_K_N]
    v_q = arena[:, A_V:A_V + A_V_N].rearrange("p (j n) -> p j n", j=6)
    P_bufs = (arena[:, A_P:A_P + A_P_N].rearrange("p (j n) -> p j n", j=4),
              arena[:, A_P2:A_P2 + A_P_N].rearrange("p (j n) -> p j n", j=4))
    att = arena[:, A_ATT:A_ATT + A_ATT_N].rearrange("p (c t) -> p c t", c=NCH)
    sq_s = arena[:, A_SQ:A_SQ + A_SQ_N]
    lnr = arena[:, A_LNR:A_LNR + A_LNR_N]
    r_mu = arena[0:1, A_LNR:A_LNR + 512]
    r_ms = arena[0:1, R_MS:R_MS + 512]
    r_var = arena[0:1, R_VAR:R_VAR + 512]
    neg_s = arena[:, A_NEG:A_NEG + A_NEG_N]
    linv = arena[:, A_LINV:A_LINV + A_LINV_N]
    msk = arena[:, A_MSK:A_MSK + A_MSK_N]

    y_b = arena[:, F_Y:F_Y + F_Y_N].rearrange("p (c t) -> p c t", c=NCH)
    h16 = arena[:, F_H:F_H + F_H_N].bitcast(BF16) \
        .rearrange("p (l c t) -> p l c t", l=2, c=NCH)
    wgu_f = arena[:, F_WGU:F_WGU + F_WGU_N].rearrange("p (s n) -> p s n", s=16)
    wgu16 = arena[:, F_WGU16:F_WGU16 + F_WGU16_N].bitcast(BF16) \
        .rearrange("p (b l s n) -> p b l s n", b=2, l=2, s=16)
    wd_f = arena[:, F_WD:F_WD + F_WD_N].rearrange("p (s n) -> p s n", s=8)
    wd16 = arena[:, F_WD16:F_WD16 + F_WD16_N].bitcast(BF16) \
        .rearrange("p (b l s n) -> p b l s n", b=2, l=2, s=8)
    xf_s = arena[:, A_XF:A_XF + 512]
    cb_b = arena[:, F_CB:F_CB + F_CB_N].rearrange("p (b n) -> p b n", b=2)
    s_s = arena[:, F_S:F_S + F_S_N]
    t_s = arena[:, F_T:F_T + F_T_N]
    fsq = arena[:, F_SQ:F_SQ + F_SQ_N]
    flnr = arena[:, F_LNR:F_LNR + F_LNR_N]

    pb = [nc.alloc_psum_tensor(f"pb{i}", [128, 512], F32).ap() for i in range(8)]

    P = Prog()
    for s in ("ld", "st", "t", "v", "s"):
        P.sem(s)

    def dma(src_ap, dst_ap, key=None, sem="ld"):
        if key is not None:
            P.acquire("sync", key)
        return P.op("sync", lambda e, S, a=src_ap, b=dst_ap: e.dma_start(out=b, in_=a),
                    inc=sem, inc_by=16)

    def mm(out_ap, lhsT, rhs, start, stop, inc=None):
        return P.op("tensor",
                    lambda e, S, o=out_ap, l=lhsT, r=rhs, st=start, sp=stop:
                    e.matmul(o, l, r, start=st, stop=sp),
                    inc=("t" if inc else None))

    dma(cstD, cst)
    dma(gwD, gw)
    dma(ebD, eb)
    dma(bqkD, bqk)
    dma(bvD, bv)
    dma(bpD, bp)
    c_msk = dma(mskD, msk[0:1, :])
    c_init = c_msk

    if PHASES == "min":
        P.wait("sync", "ld", P.counts["ld"])
        P.op("sync", lambda e, S: e.dma_start(out=outD, in_=x1.rearrange("p c t -> p (c t)")),
             inc="st", inc_by=16)
        P.wait("sync", "st", P.counts["st"])
        return _finish(nc, P, sems_names=("ld", "st", "t", "v", "s"))

    nqtr_run = {"q1": 1, "q2": 2}.get(PHASES, NQTR)
    for qtr in range(nqtr_run):
        w0 = qtr * QTR
        c_x0 = dma(x0[:, :].rearrange("p (c t) -> p c t", c=NCH)[:, :, w0:w0 + 768],
                   x0q, key="x0q")

        for quad in range(int(PHASES[1]) if PHASES.startswith("V") else 2):
            c_wv = dma(wqD[:, WQ_MAP[("v", quad)]:WQ_MAP[("v", quad)] + 4096],
                       wv_b.rearrange("p c n -> p (c n)"), key="wv")
            P.wait("tensor", "ld", c_wv)
            P.wait("tensor", "ld", c_x0)
            for j in range(6):
                P.acquire("tensor", "pb0")
                for i in range(NCH):
                    t = mm(pb[0][:, :], x0q[:, i, j * 128:(j + 1) * 128],
                           wv_b[:, i, :], start=(i == 0), stop=(i == NCH - 1),
                           inc=(i == NCH - 1))
                P.consumed("wv", "t", t)
                P.acquire("vector", f"v_q{j}")
                P.wait("vector", "t", t)
                cv = P.op("vector", lambda e, S, j=j, quad=quad:
                          e.tensor_tensor(out=v_q[:, j, :], in0=pb[0][:, :],
                                          in1=bv[:, quad * 512:(quad + 1) * 512],
                                          op=OP.add), inc="v")
                P.consumed("pb0", "v", cv)

            for pq in range(0 if PHASES.startswith("V") else 4):
                p = quad * 4 + pq
                par_b = p % 2
                off = WQ_MAP[("qk", p)]
                c_qk = dma(wqD[:, off:off + 2048],
                           wqk_b[:, par_b, :, :].rearrange("p s n -> p (s n)"),
                           key=f"wqk{par_b}")
                P.wait("tensor", "ld", c_qk)
                P.acquire("tensor", "pb1")
                for i in range(NCH):
                    t = mm(pb[1][:, :], wqk_b[:, par_b, i, :],
                           x0q[:, i, 128:640], start=(i == 0), stop=(i == NCH - 1),
                           inc=(i == NCH - 1))
                P.acquire("scalar", "q_p")
                P.wait("scalar", "t", t)
                cq = P.op("scalar", lambda e, S, p=p:
                          e.activation(out=q_p[:, :], in_=pb[1][:, :], func=AF.Identity,
                                       bias=bqk[:, 2 * p:2 * p + 1], scale=1.0),
                          inc="s")
                P.consumed("pb1", "s", cq)
                P.acquire("tensor", "pb1")
                for i in range(NCH):
                    t = mm(pb[1][:, :], wqk_b[:, par_b, 8 + i, :],
                           x0q[:, i, 0:512], start=(i == 0), stop=(i == NCH - 1),
                           inc=(i == NCH - 1))
                P.acquire("scalar", "k_p")
                P.wait("scalar", "t", t)
                ck = P.op("scalar", lambda e, S, p=p:
                          e.activation(out=k_p[:, 0:512], in_=pb[1][:, :], func=AF.Identity,
                                       bias=bqk[:, 2 * p + 1:2 * p + 2], scale=1.0),
                          inc="s")
                P.consumed("pb1", "s", ck)
                P.acquire("tensor", "pb2")
                for i in range(NCH):
                    t = mm(pb[2][:, 0:256], wqk_b[:, par_b, 8 + i, :],
                           x0q[:, i, 512:768], start=(i == 0), stop=(i == NCH - 1),
                           inc=(i == NCH - 1))
                P.consumed(f"wqk{par_b}", "t", t)
                P.wait("scalar", "t", t)
                ck2 = P.op("scalar", lambda e, S, p=p:
                           e.activation(out=k_p[:, 512:768], in_=pb[2][:, 0:256],
                                        func=AF.Identity,
                                        bias=bqk[:, 2 * p + 1:2 * p + 2], scale=1.0),
                           inc="s")
                P.consumed("pb2", "s", ck2)

                units_on = not PHASES.startswith("QK")
                ustage = int(PHASES[1]) if PHASES.startswith("U") else 99
                ust = {}

                def emit_ST(i, p=p, qtr=qtr):
                    g2, par = i // 2, i % 2
                    rows = slice(par * 64, par * 64 + 64)
                    q = i % 2
                    bA, bB = (pb[3], pb[4]) if q == 0 else (pb[6], pb[7])
                    P.acquire("tensor", f"pb{3 if q == 0 else 6}")
                    P.acquire("tensor", f"pb{4 if q == 0 else 7}")
                    t_last = None
                    for j in range(4):
                        bank = bA if j < 2 else bB
                        seg = bank[:, (j % 2) * 256:(j % 2) * 256 + 256]
                        lt = 2 * g2 + j
                        gt = 4 * qtr + 2 * g2 + j
                        edge = (gt == 0) or (gt == 17)
                        r = mm(seg, k_p[rows, lt * 128:(lt + 1) * 128],
                               q_p[rows, g2 * 256:g2 * 256 + 256],
                               start=True, stop=not edge,
                               inc=(j == 3 and not edge))
                        if edge:
                            mc = 0 if gt == 0 else 128
                            r = mm(seg, msk[0:1, mc:mc + 128],
                                   cst[0:1, 0:256], start=False, stop=True,
                                   inc=(j == 3))
                        if j == 3:
                            t_last = r
                    ust[i] = t_last

                def emit_stats(i):
                    q = i % 2
                    c0 = 4 * q
                    bA, bB = (pb[3], pb[4]) if q == 0 else (pb[6], pb[7])
                    sb = pb[0] if q == 0 else pb[1]
                    P.wait("vector", "t", ust[i])
                    P.op("vector", lambda e, S, bA=bA, c0=c0:
                         e.reduce_max(out=neg_s[:, c0:c0 + 1],
                                      in_=bA[:, :].rearrange("p (a b) -> p a b", a=2),
                                      axis=AX.XY))
                    cr = P.op("vector", lambda e, S, bB=bB, c0=c0:
                              e.reduce_max(out=neg_s[:, c0 + 1:c0 + 2],
                                           in_=bB[:, :].rearrange("p (a b) -> p a b", a=2),
                                           axis=AX.XY), inc="v")
                    P.consumed(f"pb{3 if q == 0 else 6}", "v", cr)
                    P.consumed(f"pb{4 if q == 0 else 7}", "v", cr)
                    P.wait("vector", "v", cr)
                    cm = P.op("vector", lambda e, S, c0=c0:
                              e.tensor_tensor(out=neg_s[:, c0 + 2:c0 + 3],
                                              in0=neg_s[:, c0:c0 + 1],
                                              in1=neg_s[:, c0 + 1:c0 + 2],
                                              op=OP.max), inc="v")
                    P.acquire("tensor", f"pb{0 if q == 0 else 1}")
                    P.wait("tensor", "v", cm)
                    tt1 = P.op("tensor", lambda e, S, sb=sb, c0=c0:
                               e.transpose(sb[0:1, 384:512], neg_s[:, c0 + 2:c0 + 3],
                                           ident), inc="t")
                    P.wait("vector", "t", tt1)
                    cm2 = P.op("vector", lambda e, S, sb=sb, c0=c0:
                               e.reduce_max(out=neg_s[0:1, c0 + 3:c0 + 4],
                                            in_=sb[0:1, 384:512],
                                            axis=AX.X, negate=True), inc="v")
                    P.wait("tensor", "v", cm2)
                    tt2 = P.op("tensor", lambda e, S, sb=sb, c0=c0:
                               e.matmul(sb[:, 383:384], cst[0:1, 0:128],
                                        neg_s[0:1, c0 + 3:c0 + 4],
                                        start=True, stop=True), inc="t")
                    P.wait("vector", "t", tt2)
                    cng = P.op("vector", lambda e, S, sb=sb, q=q:
                               e.tensor_scalar(out=neg_s[:, 8 + q:9 + q],
                                               in0=sb[:, 383:384],
                                               scalar1=SCALE, scalar2=None,
                                               op0=OP.mult), inc="v")
                    P.consumed(f"pb{0 if q == 0 else 1}", "v", cng)
                    ust[(i, "cng")] = cng

                def emit_exp(i):
                    q = i % 2
                    bA, bB = (pb[3], pb[4]) if q == 0 else (pb[6], pb[7])
                    P.acquire("scalar", f"P{q}")
                    P.wait("scalar", "v", ust[(i, "cng")])
                    cexp = None
                    for j in range(4):
                        bank = bA if j < 2 else bB
                        seg = bank[:, (j % 2) * 256:(j % 2) * 256 + 256]
                        cexp = P.op("scalar", lambda e, S, j=j, seg=seg, q=q:
                                    e.activation(out=P_bufs[q][:, j, :], in_=seg,
                                                 func=AF.Exp,
                                                 bias=neg_s[:, 8 + q:9 + q],
                                                 scale=SCALE), inc="s")
                    P.consumed(f"pb{3 if q == 0 else 6}", "s", cexp)
                    P.consumed(f"pb{4 if q == 0 else 7}", "s", cexp)
                    ust[(i, "exp")] = cexp

                def emit_lPV(i, p=p):
                    g2, par = i // 2, i % 2
                    q = i % 2
                    rows = slice(par * 64, par * 64 + 64)
                    tile = pb[5] if g2 == 0 else pb[2]
                    tkey = "pb5" if g2 == 0 else "pb2"
                    P.wait("tensor", "s", ust[(i, "exp")])
                    if par == 0:
                        P.acquire("tensor", tkey)
                    for j in range(4):
                        mm(tile[rows, 0:256], ones_col, P_bufs[q][:, j, :],
                           start=(j == 0), stop=(j == 3))
                    lt0 = 2 * g2
                    hq = p % 4
                    dvs = slice((2 * hq + par) * 64, (2 * hq + par) * 64 + 64)
                    t_pv = None
                    for j in range(4):
                        t_pv = mm(tile[rows, 256:512], v_q[:, lt0 + j, dvs],
                                  P_bufs[q][:, j, :], start=(j == 0), stop=(j == 3),
                                  inc=(j == 3))
                    P.consumed(f"P{q}", "t", t_pv)
                    for j in range(4):
                        P.consumed(f"v_q{lt0 + j}", "t", t_pv)
                    P.consumed("q_p", "t", t_pv)
                    P.consumed("k_p", "t", t_pv)
                    ust[(i, "pv")] = t_pv

                def emit_norm(g2, p=p):
                    tile = pb[5] if g2 == 0 else pb[2]
                    tkey = "pb5" if g2 == 0 else "pb2"
                    P.wait("vector", "t", ust[(2 * g2 + 1, "pv")])
                    cr2 = P.op("vector", lambda e, S, tile=tile:
                               e.reciprocal(out=linv[:, :], in_=tile[:, 0:256]),
                               inc="v")
                    P.wait("vector", "v", cr2)
                    cn = P.op("vector", lambda e, S, p=p, g2=g2, tile=tile:
                              e.tensor_tensor(out=att[:, p, g2 * 256:g2 * 256 + 256],
                                              in0=tile[:, 256:512], in1=linv[:, :],
                                              op=OP.mult), inc="v")
                    P.consumed(tkey, "v", cn)

                if units_on:
                    emit_ST(0)
                    if ustage > 1:
                        emit_ST(1)
                        for i in range(4):
                            emit_stats(i)
                            if ustage > 3:
                                emit_exp(i)
                                if i + 2 < 4:
                                    emit_ST(i + 2)
                                if ustage > 5:
                                    emit_lPV(i)
                                    if i % 2 == 1 and ustage > 6:
                                        emit_norm(i // 2)
        if PHASES.startswith(("V", "QK", "U")):
            break
        if DEBUG_DUMP:
            P.wait("sync", "v", P.counts["v"])
            P.op("sync", lambda e, S, qtr=qtr:
                 e.dma_start(out=dbgD.rearrange("p (c t) -> p c t", c=NCH)[:, :, qtr * QTR:(qtr + 1) * QTR],
                             in_=att.rearrange("p c t -> p (c t)")),
                 inc="st", inc_by=16)

        for o in range(NCH):
            par_b = o % 2
            c_wp = dma(wpD[:, o * 1024:(o + 1) * 1024],
                       wp_b[:, par_b, :, :].rearrange("p s n -> p (s n)"),
                       key=f"wp{par_b}")
            P.wait("tensor", "ld", c_wp)
            bank = pb[1 + par_b]
            P.acquire("tensor", f"pb{1 + par_b}")
            for i in range(NCH):
                t = mm(bank[:, :], wp_b[:, par_b, i, :], att[:, i, :],
                       start=(i == 0), stop=(i == NCH - 1), inc=(i == NCH - 1))
            P.consumed(f"wp{par_b}", "t", t)
            if o == NCH - 1:
                P.consumed("att_done", "t", t)
            P.wait("vector", "t", t)
            cz = P.op("vector", lambda e, S, o=o, bank=bank:
                      e.tensor_tensor(out=zv[:, o, :], in0=bank[:, :],
                                      in1=x0q[:, o, 128:640], op=OP.add), inc="v")
            P.consumed(f"pb{1 + par_b}", "v", cz)
            P.wait("scalar", "v", cz)
            cz2 = P.op("scalar", lambda e, S, o=o:
                       e.activation(out=zv[:, o, :], in_=zv[:, o, :], func=AF.Identity,
                                    bias=bp[:, o:o + 1], scale=1.0), inc="s")
        P.acquire("tensor", "pb0")
        P.acquire("tensor", "pb7")
        for o in range(NCH):
            P.acquire("scalar", "sq_s")
            csq = P.op("scalar", lambda e, S, o=o:
                       e.activation(out=sq_s[:, :], in_=zv[:, o, :], func=AF.Square),
                       inc="s")
            P.wait("tensor", "s", csq)
            mm(pb[0][0:1, :], cst[:, 0:1], zv[:, o, :], start=(o == 0),
               stop=(o == NCH - 1))
            t = mm(pb[7][0:1, :], cst[:, 0:1], sq_s[:, :], start=(o == 0),
                   stop=(o == NCH - 1), inc=True)
            P.consumed("sq_s", "t", t)
        P.wait("vector", "t", t)
        P.op("vector", lambda e, S:
             e.tensor_scalar(out=r_mu, in0=pb[0][0:1, :], scalar1=1.0 / D,
                             scalar2=None, op0=OP.mult))
        cv = P.op("vector", lambda e, S:
                  e.tensor_scalar(out=r_ms, in0=pb[7][0:1, :], scalar1=1.0 / D,
                                  scalar2=None, op0=OP.mult), inc="v")
        P.consumed("pb0", "v", cv)
        P.consumed("pb7", "v", cv)
        P.op("vector", lambda e, S:
             e.tensor_tensor(out=r_var, in0=r_mu, in1=r_mu, op=OP.mult))
        P.op("vector", lambda e, S:
             e.tensor_tensor(out=r_var, in0=r_ms, in1=r_var, op=OP.subtract))
        cvar = P.op("vector", lambda e, S:
                    e.tensor_scalar(out=r_var, in0=r_var, scalar1=EPS,
                                    scalar2=None, op0=OP.add), inc="v")
        P.wait("scalar", "v", cvar)
        csd = P.op("scalar", lambda e, S:
                   e.activation(out=r_var, in_=r_var, func=AF.Sqrt),
                   inc="s")
        rinv_row = arena[0:1, A_P:A_P + 512]
        arow_row = arena[0:1, A_P + 512:A_P + 1024]
        P.wait("vector", "s", csd)
        P.op("vector", lambda e, S, r=rinv_row: e.reciprocal(out=r, in_=r_var))
        crw = P.op("vector", lambda e, S, r=rinv_row, a=arow_row:
                   e.tensor_tensor(out=a, in0=r_mu, in1=r,
                                   op=OP.mult), inc="v")
        P.acquire("tensor", "pb1")
        P.acquire("tensor", "pb2")
        P.wait("tensor", "v", crw)
        mm(pb[1][:, :], cst[0:1, 0:128], rinv_row, start=True, stop=True)
        t = mm(pb[2][:, :], cst[0:1, 0:128], arow_row, start=True, stop=True,
               inc=True)
        P.consumed("P0", "v", crw)
        P.consumed("P0", "t", P.counts["t"])
        P.wait("vector", "t", t)
        for o in range(NCH):
            qsl = slice(qtr * QTR, (qtr + 1) * QTR)
            P.op("vector", lambda e, S, o=o:
                 e.tensor_tensor(out=sq_s[:, :], in0=zv[:, o, :], in1=pb[1][:, :],
                                 op=OP.mult))
            P.op("vector", lambda e, S, o=o:
                 e.tensor_tensor(out=xf_s[:, :], in0=sq_s[:, :], in1=pb[2][:, :],
                                 op=OP.subtract))
            P.op("vector", lambda e, S, o=o, qsl=qsl:
                 e.tensor_copy(out=x1h[:, o, qsl], in_=xf_s[:, :]))
            cx1 = P.op("vector", lambda e, S, o=o, qsl=qsl:
                       e.tensor_tensor(out=x1l[:, o, qsl], in0=xf_s[:, :],
                                       in1=x1h[:, o, qsl], op=OP.subtract), inc="v")
        P.consumed("pb1", "v", cx1)
        P.consumed("pb2", "v", cx1)
        P.consumed("sq_s", "v", cx1)
        P.consumed("x0q", "v", cx1)
        P.consumed("x0q", "t", P.counts["t"])
    if PHASES.startswith(("V", "QK", "U")):
        P.wait("sync", "t", P.counts["t"])
        P.op("sync", lambda e, S: e.dma_start(out=outD, in_=x1.rearrange("p c t -> p (c t)")),
             inc="st", inc_by=16)
        P.wait("sync", "st", P.counts["st"])
        return _finish(nc, P, sems_names=("ld", "st", "t", "v", "s"))
    c_x1_done = cx1

    if PHASES == "route_marker":
        pass
    r_sc_all = arena[:, F_H:F_H + 256]
    r_sel_all = arena[:, F_H + 256:F_H + 512]
    r_cmp = arena[:, F_H + 512:F_H + 768]
    r_accA = arena[:, F_H + 768:F_H + 1024]
    r_accB = arena[:, F_H + 1024:F_H + 1280]
    r_fin = arena[:, F_H + 1280:F_H + 1536]
    r_sum = arena[:, F_H + 1536:F_H + 1552]
    r_inv = arena[:, F_H + 1552:F_H + 1568]
    r_srs = arena[:, F_H + 1568:F_H + 1584]
    sel3 = r_sel_all.rearrange("p (a b) -> p a b", a=16)
    fin3 = r_fin.rearrange("p (a b) -> p a b", a=16)

    P.wait("vector", "ld", c_init)
    P.op("vector", lambda e, S: e.tensor_copy(out=gwh[:, :], in_=gw[:, :]))
    cgw = P.op("vector", lambda e, S:
               e.tensor_tensor(out=gwl[:, :], in0=gw[:, :], in1=gwh[:, :],
                               op=OP.subtract), inc="v")
    P.wait("tensor", "v", c_x1_done)
    P.wait("tensor", "v", cgw)
    P.acquire("tensor", "pb6")
    for tt in range(16):
        t0 = tt * 128
        xgw = ((x1h, gwh), (x1h, gwl), (x1l, gwh))
        for pi, (xa, ga) in enumerate(xgw):
            for i in range(NCH):
                t = mm(pb[6][:, tt * 16:(tt + 1) * 16], xa[:, i, t0:t0 + 128],
                       ga[:, i * 16:(i + 1) * 16],
                       start=(pi == 0 and i == 0),
                       stop=(pi == 2 and i == NCH - 1),
                       inc=(tt == 15 and pi == 2 and i == NCH - 1))
    P.wait("scalar", "t", t)
    P.wait("scalar", "ld", c_init)
    cs = P.op("scalar", lambda e, S:
              e.activation(out=r_sc_all, in_=pb[6][:, 0:256], func=AF.Sigmoid),
              inc="s")
    P.consumed("pb6", "s", cs)
    P.wait("vector", "s", cs)
    P.op("vector", lambda e, S:
         e.tensor_tensor(out=r_sel_all.rearrange("p (a b) -> p a b", a=16),
                         in0=r_sc_all.rearrange("p (a b) -> p a b", a=16),
                         in1=eb[:, :].unsqueeze(1).broadcast_to([128, 16, 16]),
                         op=OP.add))
    accs = (r_accA, r_accB)
    for j in range(NE):
        P.op("vector", lambda e, S, j=j:
             e.tensor_tensor(out=r_cmp.rearrange("p (a b) -> p a b", a=16),
                             in0=sel3,
                             in1=sel3[:, :, j:j + 1].broadcast_to([128, 16, 16]),
                             op=OP.is_lt))
        if j == 0:
            P.op("vector", lambda e, S: e.tensor_copy(out=r_accA, in_=r_cmp))
        else:
            P.op("vector", lambda e, S, o=accs[j % 2], i0=accs[(j + 1) % 2]:
                 e.tensor_tensor(out=o, in0=i0, in1=r_cmp, op=OP.add))
    r_rank = accs[(NE - 1) % 2]
    P.op("vector", lambda e, S, a=r_rank:
         e.tensor_scalar(out=r_cmp, in0=a, scalar1=7.5, scalar2=None, op0=OP.is_le))
    P.op("vector", lambda e, S:
         e.tensor_tensor(out=r_fin, in0=r_sc_all, in1=r_cmp, op=OP.mult))
    c1 = P.op("vector", lambda e, S:
              e.reduce_sum(out=r_sum.rearrange("p (a b) -> p a b", b=1),
                           in_=fin3, axis=AX.X), inc="v")
    P.wait("vector", "v", c1)
    c2 = P.op("vector", lambda e, S: e.reciprocal(out=r_inv, in_=r_sum), inc="v")
    P.wait("vector", "v", c2)
    c3 = P.op("vector", lambda e, S:
              e.tensor_scalar(out=r_srs, in0=r_inv, scalar1=RSCALE, scalar2=None,
                              op0=OP.mult), inc="v")
    P.wait("vector", "v", c3)
    cw = P.op("vector", lambda e, S:
              e.tensor_tensor(out=fin3, in0=fin3,
                              in1=r_srs.rearrange("p (a b) -> p a b", b=1)
                              .broadcast_to([128, 16, 16]), op=OP.mult), inc="v")
    for tt in range(16):
        P.acquire("tensor", "pb7")
        P.wait("tensor", "v", cw)
        t = P.op("tensor", lambda e, S, tt=tt:
                 e.transpose(pb[7][0:16, 0:128], fin3[:, tt, :], ident), inc="t")
        P.wait("vector", "t", t)
        cc = P.op("vector", lambda e, S, tt=tt:
                  e.tensor_copy(out=comb[0:16, tt * 128:(tt + 1) * 128],
                                in_=pb[7][0:16, 0:128]), inc="v")
        P.consumed("pb7", "v", cc)
    c_comb_done = cc

    if PHASES == "route":
        P.wait("sync", "v", c_comb_done)
        P.op("sync", lambda e, S: e.dma_start(out=outD, in_=x1.rearrange("p c t -> p (c t)")),
             inc="st", inc_by=16)
        if DEBUG_DUMP:
            P.op("sync", lambda e, S: e.dma_start(out=dbgD[0:16, 0:TPC], in_=comb),
                 inc="st", inc_by=16)
            for i, tsr in enumerate((r_sc, r_sel, r_acc, r_mk, r_mk, r_w, r_s)):
                P.op("sync", lambda e, S, i=i, tsr=tsr:
                     e.dma_start(out=dbgD[:, TPC + i * 16:TPC + i * 16 + tsr.shape[-1]],
                                 in_=tsr), inc="st", inc_by=16)
        P.wait("sync", "st", P.counts["st"])
        return _finish(nc, P, sems_names=("ld", "st", "t", "v", "s"))

    for hf in range(NHF):
        h0 = hf * HF
        for e_i in range(NEX):
            e_off = e_i * WFFN_E_COLS
            routed = e_i < NE
            if routed:
                P.wait("tensor", "v", c_comb_done)
                for tt in range(2):
                    P.acquire("tensor", "pb6")
                    t = P.op("tensor", lambda e, S, e_i=e_i, tt=tt, h0=h0:
                             e.matmul(pb[6][:, :], sel_c[0:16, e_i * 128:(e_i + 1) * 128],
                                      comb[0:16, h0 + tt * 512:h0 + (tt + 1) * 512],
                                      start=True, stop=True), inc="t")
                    P.acquire("vector", f"cb{tt}")
                    P.wait("vector", "t", t)
                    ccb = P.op("vector", lambda e, S, tt=tt:
                               e.tensor_copy(out=cb_b[:, tt, :], in_=pb[6][:, :]),
                               inc="v")
                    P.consumed("pb6", "v", ccb)
            def wgu_load_split(cc, e_off=e_off):
                par_b = cc % 2
                c_w = dma(wffnD[:, e_off + cc * 2048:e_off + (cc + 1) * 2048],
                          wgu_f.rearrange("p s n -> p (s n)"), key="wgu_f")
                P.acquire("scalar", f"wgu16_{par_b}")
                P.wait("scalar", "ld", c_w)
                cca = P.op("scalar", lambda e, S, par_b=par_b:
                           e.activation(out=wgu16[:, par_b, 0, :, :].rearrange("p s n -> p (s n)"),
                                        in_=wgu_f.rearrange("p s n -> p (s n)"),
                                        func=AF.Identity), inc="s")
                P.wait("vector", "s", cca)
                csp = P.op("vector", lambda e, S, par_b=par_b:
                           e.tensor_tensor(out=wgu16[:, par_b, 1, :, :].rearrange("p s n -> p (s n)"),
                                           in0=wgu_f.rearrange("p s n -> p (s n)"),
                                           in1=wgu16[:, par_b, 0, :, :].rearrange("p s n -> p (s n)"),
                                           op=OP.subtract), inc="v")
                P.consumed("wgu_f", "v", csp)
                return cca, csp

            spl = {0: wgu_load_split(0)}
            for c in range(NCH):
                par_b = c % 2
                if c + 1 < NCH:
                    spl[c + 1] = wgu_load_split(c + 1)
                cca, csp = spl[c]
                P.wait("tensor", "s", cca)
                P.wait("tensor", "v", csp)
                for tt in range(2):
                    ts0 = h0 + tt * 512
                    par_p = (c * 2 + tt) % 2
                    gbank = pb[par_p]
                    ubank = pb[2 + par_p]
                    passes = ((0, x1h), (1, x1h), (0, x1l))
                    P.acquire("tensor", f"pb{par_p}")
                    for pi, (wl, xa) in enumerate(passes):
                        for i in range(NCH):
                            mm(gbank[:, :], wgu16[:, par_b, wl, i, :],
                               xa[:, i, ts0:ts0 + 512], start=(pi == 0 and i == 0),
                               stop=(pi == 2 and i == NCH - 1))
                    P.acquire("tensor", f"pb{2 + par_p}")
                    for pi, (wl, xa) in enumerate(passes):
                        for i in range(NCH):
                            t = mm(ubank[:, :], wgu16[:, par_b, wl, 8 + i, :],
                                   xa[:, i, ts0:ts0 + 512], start=(pi == 0 and i == 0),
                                   stop=(pi == 2 and i == NCH - 1),
                                   inc=(pi == 2 and i == NCH - 1))
                    P.consumed(f"wgu16_{par_b}", "t", t)
                    P.acquire("scalar", "s_s")
                    P.wait("scalar", "t", t)
                    csl = P.op("scalar", lambda e, S, gbank=gbank:
                               e.activation(out=s_s[:, :], in_=gbank[:, :],
                                            func=AF.Silu), inc="s")
                    P.consumed(f"pb{par_p}", "s", csl)
                    P.wait("vector", "t", t)
                    P.wait("vector", "s", csl)
                    hsl = slice(tt * 512, (tt + 1) * 512)
                    if routed:
                        P.op("vector", lambda e, S, ubank=ubank, tt=tt:
                             e.tensor_tensor(out=t_s[:, :], in0=ubank[:, :],
                                             in1=cb_b[:, tt, :], op=OP.mult))
                        cf = P.op("vector", lambda e, S:
                                  e.tensor_tensor(out=s_s[:, :],
                                                  in0=s_s[:, :], in1=t_s[:, :],
                                                  op=OP.mult), inc="v")
                    else:
                        cf = P.op("vector", lambda e, S, ubank=ubank:
                                  e.tensor_tensor(out=s_s[:, :],
                                                  in0=s_s[:, :], in1=ubank[:, :],
                                                  op=OP.mult), inc="v")
                    P.consumed(f"pb{2 + par_p}", "v", cf)
                    P.wait("scalar", "v", cf)
                    chh = P.op("scalar", lambda e, S, c=c, hsl=hsl:
                               e.activation(out=h16[:, 0, c, hsl], in_=s_s[:, :],
                                            func=AF.Identity), inc="s")
                    P.wait("vector", "s", chh)
                    ch = P.op("vector", lambda e, S, c=c, hsl=hsl:
                              e.tensor_tensor(out=h16[:, 1, c, hsl], in0=s_s[:, :],
                                              in1=h16[:, 0, c, hsl],
                                              op=OP.subtract), inc="v")
                    P.consumed("s_s", "v", ch)
            c_h_done = ch
            def wd_load_split(oo, e_off=e_off):
                par_b = oo % 2
                c_w = dma(wffnD[:, e_off + 16384 + oo * 1024:e_off + 16384 + (oo + 1) * 1024],
                          wd_f.rearrange("p s n -> p (s n)"), key="wd_f")
                P.acquire("scalar", f"wd16_{par_b}")
                P.wait("scalar", "ld", c_w)
                cca = P.op("scalar", lambda e, S, par_b=par_b:
                           e.activation(out=wd16[:, par_b, 0, :, :].rearrange("p s n -> p (s n)"),
                                        in_=wd_f.rearrange("p s n -> p (s n)"),
                                        func=AF.Identity), inc="s")
                P.wait("vector", "s", cca)
                csp = P.op("vector", lambda e, S, par_b=par_b:
                           e.tensor_tensor(out=wd16[:, par_b, 1, :, :].rearrange("p s n -> p (s n)"),
                                           in0=wd_f.rearrange("p s n -> p (s n)"),
                                           in1=wd16[:, par_b, 0, :, :].rearrange("p s n -> p (s n)"),
                                           op=OP.subtract), inc="v")
                P.consumed("wd_f", "v", csp)
                return cca, csp

            dspl = {0: wd_load_split(0)}
            for o in range(NCH):
                par_b = o % 2
                if o + 1 < NCH:
                    dspl[o + 1] = wd_load_split(o + 1)
                cca, csp = dspl[o]
                P.wait("tensor", "s", cca)
                P.wait("tensor", "v", csp)
                P.wait("tensor", "v", c_h_done)
                for tt in range(2):
                    par_p = (o * 2 + tt) % 2
                    ybank = pb[4 + par_p]
                    hsl = slice(tt * 512, (tt + 1) * 512)
                    P.acquire("tensor", f"pb{4 + par_p}")
                    passes = ((0, 0), (1, 0), (0, 1))
                    for pi, (wl, hl) in enumerate(passes):
                        for c in range(NCH):
                            t = mm(ybank[:, :], wd16[:, par_b, wl, c, :],
                                   h16[:, hl, c, hsl], start=(pi == 0 and c == 0),
                                   stop=(pi == 2 and c == NCH - 1),
                                   inc=(pi == 2 and c == NCH - 1))
                    P.consumed(f"wd16_{par_b}", "t", t)
                    if e_i == NEX - 1:
                        P.consumed("h16", "t", t)
                    P.wait("vector", "t", t)
                    if e_i == 0:
                        P.op("vector", lambda e, S, o=o, tt=tt, h0=h0, ybank=ybank:
                             e.tensor_tensor(out=y_b[:, o, tt * 512:(tt + 1) * 512],
                                             in0=ybank[:, :],
                                             in1=x1h[:, o, h0 + tt * 512:h0 + (tt + 1) * 512],
                                             op=OP.add))
                        cy = P.op("vector", lambda e, S, o=o, tt=tt, h0=h0:
                                  e.tensor_tensor(out=y_b[:, o, tt * 512:(tt + 1) * 512],
                                                  in0=y_b[:, o, tt * 512:(tt + 1) * 512],
                                                  in1=x1l[:, o, h0 + tt * 512:h0 + (tt + 1) * 512],
                                                  op=OP.add), inc="v")
                    else:
                        cy = P.op("vector", lambda e, S, o=o, tt=tt, ybank=ybank:
                                  e.tensor_tensor(out=y_b[:, o, tt * 512:(tt + 1) * 512],
                                                  in0=y_b[:, o, tt * 512:(tt + 1) * 512],
                                                  in1=ybank[:, :], op=OP.add), inc="v")
                    P.consumed(f"pb{4 + par_p}", "v", cy)
        for tt in range(2):
            tsl = slice(tt * 512, (tt + 1) * 512)
            P.acquire("tensor", "pb6")
            P.acquire("tensor", "pb7")
            P.wait("scalar", "v", P.counts["v"])
            for o in range(NCH):
                P.acquire("scalar", "fsq")
                csq = P.op("scalar", lambda e, S, o=o, tsl=tsl:
                           e.activation(out=fsq[:, :], in_=y_b[:, o, tsl],
                                        func=AF.Square), inc="s")
                P.wait("tensor", "s", csq)
                mm(pb[6][0:1, :], cst[:, 0:1], y_b[:, o, tsl], start=(o == 0),
                   stop=(o == NCH - 1))
                t = mm(pb[7][0:1, :], cst[:, 0:1], fsq[:, :], start=(o == 0),
                       stop=(o == NCH - 1), inc=True)
                P.consumed("fsq", "t", t)
            P.wait("vector", "t", t)
            P.op("vector", lambda e, S:
                 e.tensor_scalar(out=r_mu, in0=pb[6][0:1, :], scalar1=1.0 / D,
                                 scalar2=None, op0=OP.mult))
            cv = P.op("vector", lambda e, S:
                      e.tensor_scalar(out=r_ms, in0=pb[7][0:1, :],
                                      scalar1=1.0 / D, scalar2=None, op0=OP.mult),
                      inc="v")
            P.consumed("pb6", "v", cv)
            P.consumed("pb7", "v", cv)
            P.op("vector", lambda e, S:
                 e.tensor_tensor(out=r_var, in0=r_mu, in1=r_mu, op=OP.mult))
            P.op("vector", lambda e, S:
                 e.tensor_tensor(out=r_var, in0=r_ms, in1=r_var, op=OP.subtract))
            cvar = P.op("vector", lambda e, S:
                        e.tensor_scalar(out=r_var, in0=r_var, scalar1=EPS,
                                        scalar2=None, op0=OP.add), inc="v")
            P.wait("scalar", "v", cvar)
            csd = P.op("scalar", lambda e, S:
                       e.activation(out=r_var, in_=r_var, func=AF.Sqrt),
                       inc="s")
            rinv2 = arena[0:1, F_CB:F_CB + 512]
            arow2 = arena[0:1, F_CB + 512:F_CB + 1024]
            P.wait("vector", "s", csd)
            P.op("vector", lambda e, S, r=rinv2: e.reciprocal(out=r, in_=r_var))
            crw = P.op("vector", lambda e, S, r=rinv2, a=arow2:
                       e.tensor_tensor(out=a, in0=r_mu,
                                       in1=r, op=OP.mult), inc="v")
            P.acquire("tensor", "pb0")
            P.acquire("tensor", "pb1")
            P.wait("tensor", "v", crw)
            mm(pb[0][:, :], cst[0:1, 0:128], rinv2, start=True, stop=True)
            t = mm(pb[1][:, :], cst[0:1, 0:128], arow2, start=True, stop=True,
                   inc=True)
            P.wait("vector", "t", t)
            for o in range(NCH):
                P.op("vector", lambda e, S, o=o, tsl=tsl:
                     e.tensor_tensor(out=fsq[:, :], in0=y_b[:, o, tsl], in1=pb[0][:, :],
                                     op=OP.mult))
                cz2 = P.op("vector", lambda e, S, o=o, tsl=tsl:
                           e.tensor_tensor(out=y_b[:, o, tsl], in0=fsq[:, :],
                                           in1=pb[1][:, :], op=OP.subtract), inc="v")
            P.consumed("pb0", "v", cz2)
            P.consumed("pb1", "v", cz2)
            P.consumed("fsq", "v", cz2)
        P.wait("sync", "v", P.counts["v"])
        c_st = P.op("sync", lambda e, S, hf=hf:
                    e.dma_start(out=outD.rearrange("p (c t) -> p c t", c=NCH)[:, :, hf * HF:(hf + 1) * HF],
                                in_=y_b.rearrange("p c t -> p (c t)")),
                    inc="st", inc_by=16)
        P.consumed("y_half", "st", P.counts["st"])
        P.acquire("vector", "y_half")

    P.wait("sync", "st", P.counts["st"])
    return _finish(nc, P, sems_names=("ld", "st", "t", "v", "s"))


def _finish(nc, P, sems_names):
    sems = {}
    for name in sems_names:
        sems[name] = nc.alloc_semaphore(f"sem_{name}")
    with nc.Block() as blk:
        @blk.sync
        def _(sync):
            P.emit("sync", sync, sems)

        @blk.tensor
        def _(tensor):
            P.emit("tensor", tensor, sems)

        @blk.vector
        def _(vector):
            P.emit("vector", vector, sems)

        @blk.scalar
        def _(scalar):
            P.emit("scalar", scalar, sems)

    return nc


def _prep_shared(L0_attn_w_qkv, L0_attn_b_qkv, L0_attn_w_proj, L0_attn_b_proj,
                 L0_ffn_gate_w, L0_ffn_expert_bias, L0_ffn_w_gate, L0_ffn_w_up,
                 L0_ffn_w_down, L0_ffn_shared_gate_w, L0_ffn_shared_up_w,
                 L0_ffn_shared_down_w):
    f = np.float32
    wqkvT = np.ascontiguousarray(np.asarray(L0_attn_w_qkv, f).T)
    wq = np.empty((128, WQ_COLS), f)
    for quad in range(2):
        off = WQ_MAP[("v", quad)]
        vsl = wqkvT[:, 2048 + quad * 512:2048 + (quad + 1) * 512]
        for i in range(8):
            wq[:, off + i * 512:off + (i + 1) * 512] = vsl[i * 128:(i + 1) * 128, :]
        for pq in range(4):
            p = quad * 4 + pq
            off = WQ_MAP[("qk", p)]
            qsl = wqkvT[:, p * 128:(p + 1) * 128]
            ksl = wqkvT[:, 1024 + p * 128:1024 + (p + 1) * 128]
            for i in range(8):
                wq[:, off + i * 128:off + (i + 1) * 128] = qsl[i * 128:(i + 1) * 128, :]
                wq[:, off + 1024 + i * 128:off + 1024 + (i + 1) * 128] = \
                    ksl[i * 128:(i + 1) * 128, :]

    wpT = np.ascontiguousarray(np.asarray(L0_attn_w_proj, f).T)
    wp = np.empty((128, 8192), f)
    for o in range(8):
        for i in range(8):
            wp[:, o * 1024 + i * 128:o * 1024 + (i + 1) * 128] = \
                wpT[i * 128:(i + 1) * 128, o * 128:(o + 1) * 128]

    wffn = np.empty((128, WFFN_COLS), f)
    wg_all = np.asarray(L0_ffn_w_gate, f)
    wu_all = np.asarray(L0_ffn_w_up, f)
    wd_all = np.asarray(L0_ffn_w_down, f)
    for e in range(NEX):
        if e < NE:
            gT = wg_all[e].T
            uT = wu_all[e].T
            dT = wd_all[e].T
        else:
            gT = np.asarray(L0_ffn_shared_gate_w, f).T
            uT = np.asarray(L0_ffn_shared_up_w, f).T
            dT = np.asarray(L0_ffn_shared_down_w, f).T
        eo = e * WFFN_E_COLS
        for c in range(8):
            for i in range(8):
                wffn[:, eo + c * 2048 + i * 128:eo + c * 2048 + (i + 1) * 128] = \
                    gT[i * 128:(i + 1) * 128, c * 128:(c + 1) * 128]
                wffn[:, eo + c * 2048 + 1024 + i * 128:eo + c * 2048 + 1024 + (i + 1) * 128] = \
                    uT[i * 128:(i + 1) * 128, c * 128:(c + 1) * 128]
        do = eo + 16384
        for o in range(8):
            for c in range(8):
                wffn[:, do + o * 1024 + c * 128:do + o * 1024 + (c + 1) * 128] = \
                    dT[c * 128:(c + 1) * 128, o * 128:(o + 1) * 128]

    cstv = np.zeros((128, 2432), f)
    cstv[:, 0:256] = 1.0
    cstv[:, 256:384] = np.eye(128, dtype=f)
    for e in range(16):
        cstv[e, 384 + e * 128:384 + (e + 1) * 128] = 1.0

    gwT = np.asarray(L0_ffn_gate_w, f).T
    gwv = np.empty((128, 128), f)
    for c in range(8):
        gwv[:, c * 16:(c + 1) * 16] = gwT[c * 128:(c + 1) * 128, :]

    ebv = np.tile(np.asarray(L0_ffn_expert_bias, f)[None, :], (128, 1))

    bqkv = np.empty((128, 16), f)
    bq = np.asarray(L0_attn_b_qkv, f)
    for p in range(8):
        bqkv[:, 2 * p] = bq[p * 128:(p + 1) * 128]
        bqkv[:, 2 * p + 1] = bq[1024 + p * 128:1024 + (p + 1) * 128]
    bvv = np.tile(bq[2048:3072][None, :], (128, 1)).astype(f)

    bpv = np.empty((128, 8), f)
    bpr = np.asarray(L0_attn_b_proj, f)
    for o in range(8):
        bpv[:, o] = bpr[o * 128:(o + 1) * 128]

    return {"cst": cstv, "wq": wq, "wp": wp, "gw": gwv, "eb": ebv,
            "bqk": bqkv, "bv": bvv, "bp": bpv, "wffn": wffn}


def _core_inputs(x, c):
    f = np.float32
    b, half = c // 2, c % 2
    s0 = half * TPC
    pad = np.zeros((SEQ + 2 * EXT, D), f)
    pad[EXT:EXT + SEQ] = x[b]
    sl = pad[s0:s0 + HALO]
    x0 = np.ascontiguousarray(sl.T)
    x0 = x0.reshape(NCH, 128, HALO).transpose(1, 0, 2).reshape(128, NCH * HALO)
    pos0 = s0 - EXT + np.arange(128)
    pos17 = s0 - EXT + 17 * 128 + np.arange(128)
    mskv = np.concatenate([
        np.where((pos0 < 0) | (pos0 >= SEQ), NEGBIG, 0.0),
        np.where((pos17 < 0) | (pos17 >= SEQ), NEGBIG, 0.0)]).astype(f)[None, :]
    return {"x0": np.ascontiguousarray(x0), "msk": mskv}


def kernel(L0_attn_x, L0_attn_w_qkv, L0_attn_b_qkv, L0_attn_w_proj, L0_attn_b_proj,
           L0_ffn_gate_w, L0_ffn_expert_bias, L0_ffn_w_gate, L0_ffn_w_up,
           L0_ffn_w_down, L0_ffn_shared_gate_w, L0_ffn_shared_up_w,
           L0_ffn_shared_down_w):
    f = np.float32
    x = np.asarray(L0_attn_x, f)
    shared = _prep_shared(L0_attn_w_qkv, L0_attn_b_qkv, L0_attn_w_proj,
                          L0_attn_b_proj, L0_ffn_gate_w, L0_ffn_expert_bias,
                          L0_ffn_w_gate, L0_ffn_w_up, L0_ffn_w_down,
                          L0_ffn_shared_gate_w, L0_ffn_shared_up_w,
                          L0_ffn_shared_down_w)

    in_maps = [dict(shared, **_core_inputs(x, c)) for c in range(8)]

    nc = build_program()
    res = run_bass_kernel_spmd(nc, in_maps, core_ids=list(range(8)))

    out = np.empty((NB, SEQ, D), f)
    for c in range(8):
        b, half = c // 2, c % 2
        s0 = half * TPC
        yc = res.results[c]["y"].reshape(128, NCH, TPC).transpose(1, 0, 2)
        out[b, s0:s0 + TPC] = yc.reshape(D, TPC).T
    return out


# version 08
# speedup vs baseline: 1.0128x; 1.0128x over previous
import numpy as np
import concourse.bass as bass
import concourse.mybir as mybir
from concourse.bass_utils import run_bass_kernel_spmd
from concourse.alu_op_type import AluOpType as OP

F32 = mybir.dt.float32
AF = mybir.ActivationFunctionType
AX = mybir.AxisListType

D = 1024
NH = 16
HD = 64
WIN = 256
EXT = 128
NB = 4
SEQ = 4096
TPC = 2048
HALO = TPC + 2 * EXT
NCH = D // 128
NE = 16
NEX = 17
TOPK = 8
RSCALE = 2.826
EPS = 1e-5
NEGBIG = -1e9
SCALE = HD ** -0.5

NQTR = 4
QTR = 512
NHF = 2
HF = 1024

def wq_map():
    m = {}
    off = 0
    for quad in range(2):
        m[("v", quad)] = off
        off += 8 * 512
        for pq in range(4):
            m[("qk", quad * 4 + pq)] = off
            off += 16 * 128
    return m, off

WQ_MAP, WQ_COLS = wq_map()
WFFN_E_COLS = 16 * 1024 + 8 * 1024
WFFN_COLS = NEX * WFFN_E_COLS

A_X0Q = 0;        A_X0Q_N = 8 * 768
A_WV = 6144;      A_WV_N = 8 * 512
A_WQK = 10240;    A_WQK_N = 2 * 2048
A_WP = 14336;     A_WP_N = 2 * 1024
A_Q = 16384;      A_Q_N = 512
A_K = 16896;      A_K_N = 768
A_V = 17664;      A_V_N = 6 * 512
A_P = 20736;      A_P_N = 4 * 256
A_ATT = 21760;    A_ATT_N = 8 * 512
A_SQ = 25856;     A_SQ_N = 512
A_LNR = 26368;    A_LNR_N = 512
A_NEG = 26880;    A_NEG_N = 12
A_LINV = 26892;   A_LINV_N = 256
A_MSK = 27148;    A_MSK_N = 256
ATT_END = A_MSK + A_MSK_N

F_Y = 0;          F_Y_N = 8 * 1024
F_H = 8192;       F_H_N = 8 * 1024
F_WGU = 16384;    F_WGU_N = 2048
F_WGU16 = 18432;  F_WGU16_N = 4096
F_WD = 22528;     F_WD_N = 1024
F_WD16 = 23552;   F_WD16_N = 2048
F_CB = 25600;     F_CB_N = 2 * 512
F_S = 26624;      F_S_N = 512
F_T = 27136;      F_T_N = 512
F_SQ = 27648;     F_SQ_N = 512
F_LNR = 28160;    F_LNR_N = 512
FFN_END = F_LNR + F_LNR_N

R_MS = 28672
R_VAR = 29184
A_XF = 29696
A_P2 = 30208
ARENA_COLS = 31232

ENGINES = ("sync", "tensor", "vector", "scalar", "gpsimd")

DEBUG_DUMP = False
PHASES = "all"


class Prog:

    def __init__(self):
        self.ops = {e: [] for e in ENGINES}
        self.counts = {}
        self.guards = {}

    def sem(self, name):
        self.counts[name] = 0

    def op(self, eng, fn, inc=None, inc_by=1):
        val = None
        if inc is not None:
            self.counts[inc] += inc_by
            val = self.counts[inc]
        self.ops[eng].append(("op", fn, inc, inc_by))
        return val

    def wait(self, eng, sem, val):
        assert val <= self.counts[sem], (sem, val, self.counts[sem])
        if val > 0:
            self.ops[eng].append(("wait", sem, val))

    def consumed(self, key, sem, val):
        g = self.guards.setdefault(key, {})
        g[sem] = max(g.get(sem, 0), val)

    def acquire(self, eng, key):
        g = self.guards.pop(key, None)
        if g:
            for sem, val in g.items():
                self.wait(eng, sem, val)

    def emit(self, eng, proxy, sems):
        for item in self.ops[eng]:
            if item[0] == "wait":
                _, sem, val = item
                proxy.wait_ge(sems[sem], val)
            else:
                _, fn, inc, inc_by = item
                instr = fn(proxy, sems)
                if inc is not None:
                    instr.then_inc(sems[inc], inc_by)


def build_program():
    nc = bass.Bass("TRN2", target_bir_lowering=False, debug=False,
                   detect_race_conditions=False, dynamic_dma_scratch_size=4096)

    x0 = nc.dram_tensor("x0", [128, NCH * HALO], F32, kind="ExternalInput").ap()
    mskD = nc.dram_tensor("msk", [1, 256], F32, kind="ExternalInput").ap()
    cstD = nc.dram_tensor("cst", [128, 2432], F32, kind="ExternalInput").ap()
    wqD = nc.dram_tensor("wq", [128, WQ_COLS], F32, kind="ExternalInput").ap()
    wpD = nc.dram_tensor("wp", [128, 8192], F32, kind="ExternalInput").ap()
    gwD = nc.dram_tensor("gw", [128, 128], F32, kind="ExternalInput").ap()
    ebD = nc.dram_tensor("eb", [128, 16], F32, kind="ExternalInput").ap()
    bqkD = nc.dram_tensor("bqk", [128, 16], F32, kind="ExternalInput").ap()
    bvD = nc.dram_tensor("bv", [128, 1024], F32, kind="ExternalInput").ap()
    bpD = nc.dram_tensor("bp", [128, 8], F32, kind="ExternalInput").ap()
    wffnD = nc.dram_tensor("wffn", [128, WFFN_COLS], F32, kind="ExternalInput").ap()
    outD = nc.dram_tensor("y", [128, NCH * TPC], F32, kind="ExternalOutput").ap()
    if DEBUG_DUMP:
        dbgD = nc.dram_tensor("dbg", [128, NCH * TPC], F32, kind="ExternalOutput").ap()

    BF16 = mybir.dt.bfloat16
    arena = nc.alloc_sbuf_tensor("arena", [128, ARENA_COLS], F32).ap()
    x1h = nc.alloc_sbuf_tensor("x1h", [128, NCH, TPC], BF16).ap()
    x1l = nc.alloc_sbuf_tensor("x1l", [128, NCH, TPC], BF16).ap()
    gwh = nc.alloc_sbuf_tensor("gwh", [128, 128], BF16).ap()
    gwl = nc.alloc_sbuf_tensor("gwl", [128, 128], BF16).ap()
    cst = nc.alloc_sbuf_tensor("cst_sb", [128, 2432], F32).ap()
    gw = nc.alloc_sbuf_tensor("gw_sb", [128, 128], F32).ap()
    eb = nc.alloc_sbuf_tensor("eb_sb", [128, 16], F32).ap()
    bqk = nc.alloc_sbuf_tensor("bqk_sb", [128, 16], F32).ap()
    bv = nc.alloc_sbuf_tensor("bv_sb", [128, 1024], F32).ap()
    bp = nc.alloc_sbuf_tensor("bp_sb", [128, 8], F32).ap()
    comb = nc.alloc_sbuf_tensor("comb_sb", [16, TPC], F32).ap()
    r_sc = nc.alloc_sbuf_tensor("r_sc", [128, 16], F32).ap()
    r_sel = nc.alloc_sbuf_tensor("r_sel", [128, 16], F32).ap()
    r_acc = nc.alloc_sbuf_tensor("r_acc", [128, 16], F32).ap()
    r_thr = nc.alloc_sbuf_tensor("r_thr", [128, 1], F32).ap()
    r_mk = nc.alloc_sbuf_tensor("r_mk", [128, 16], F32).ap()
    r_w = nc.alloc_sbuf_tensor("r_w", [128, 16], F32).ap()
    r_s = nc.alloc_sbuf_tensor("r_s", [128, 2], F32).ap()

    ones_col = cst[:, 0:64]
    ident = cst[:, 256:384]
    sel_c = cst[:, 384:2432]

    x0q = arena[:, A_X0Q:A_X0Q + A_X0Q_N].rearrange("p (c t) -> p c t", c=NCH)
    zv = x0q[:, :, 0:512]
    wv_b = arena[:, A_WV:A_WV + A_WV_N].rearrange("p (c n) -> p c n", c=8)
    wqk_b = arena[:, A_WQK:A_WQK + A_WQK_N].rearrange("p (b s n) -> p b s n", b=2, s=16)
    wp_b = arena[:, A_WP:A_WP + A_WP_N].rearrange("p (b s n) -> p b s n", b=2, s=8)
    q_p = arena[:, A_Q:A_Q + A_Q_N]
    k_p = arena[:, A_K:A_K + A_K_N]
    v_q = arena[:, A_V:A_V + A_V_N].rearrange("p (j n) -> p j n", j=6)
    P_bufs = (arena[:, A_P:A_P + A_P_N].rearrange("p (j n) -> p j n", j=4),
              arena[:, A_P2:A_P2 + A_P_N].rearrange("p (j n) -> p j n", j=4))
    att = arena[:, A_ATT:A_ATT + A_ATT_N].rearrange("p (c t) -> p c t", c=NCH)
    sq_s = arena[:, A_SQ:A_SQ + A_SQ_N]
    lnr = arena[:, A_LNR:A_LNR + A_LNR_N]
    r_mu = arena[0:1, A_LNR:A_LNR + 512]
    r_ms = arena[0:1, R_MS:R_MS + 512]
    r_var = arena[0:1, R_VAR:R_VAR + 512]
    neg_s = arena[:, A_NEG:A_NEG + A_NEG_N]
    linv = arena[:, A_LINV:A_LINV + A_LINV_N]
    msk = arena[:, A_MSK:A_MSK + A_MSK_N]

    y_b = arena[:, F_Y:F_Y + F_Y_N].rearrange("p (c t) -> p c t", c=NCH)
    h16 = arena[:, F_H:F_H + F_H_N].bitcast(BF16) \
        .rearrange("p (l c t) -> p l c t", l=2, c=NCH)
    wgu_f = arena[:, F_WGU:F_WGU + F_WGU_N].rearrange("p (s n) -> p s n", s=16)
    wgu16 = arena[:, F_WGU16:F_WGU16 + F_WGU16_N].bitcast(BF16) \
        .rearrange("p (b l s n) -> p b l s n", b=2, l=2, s=16)
    wd_f = arena[:, F_WD:F_WD + F_WD_N].rearrange("p (s n) -> p s n", s=8)
    wd16 = arena[:, F_WD16:F_WD16 + F_WD16_N].bitcast(BF16) \
        .rearrange("p (b l s n) -> p b l s n", b=2, l=2, s=8)
    xf_s = arena[:, A_XF:A_XF + 512]
    cb_b = arena[:, F_CB:F_CB + F_CB_N].rearrange("p (b n) -> p b n", b=2)
    s_s = arena[:, F_S:F_S + F_S_N]
    t_s = arena[:, F_T:F_T + F_T_N]
    fsq = arena[:, F_SQ:F_SQ + F_SQ_N]
    flnr = arena[:, F_LNR:F_LNR + F_LNR_N]

    pb = [nc.alloc_psum_tensor(f"pb{i}", [128, 512], F32).ap() for i in range(8)]

    P = Prog()
    for s in ("ld", "st", "t", "v", "s"):
        P.sem(s)

    def dma(src_ap, dst_ap, key=None, sem="ld"):
        if key is not None:
            P.acquire("sync", key)
        return P.op("sync", lambda e, S, a=src_ap, b=dst_ap: e.dma_start(out=b, in_=a),
                    inc=sem, inc_by=16)

    def mm(out_ap, lhsT, rhs, start, stop, inc=None):
        return P.op("tensor",
                    lambda e, S, o=out_ap, l=lhsT, r=rhs, st=start, sp=stop:
                    e.matmul(o, l, r, start=st, stop=sp),
                    inc=("t" if inc else None))

    dma(cstD, cst)
    dma(gwD, gw)
    dma(ebD, eb)
    dma(bqkD, bqk)
    dma(bvD, bv)
    dma(bpD, bp)
    c_msk = dma(mskD, msk[0:1, :])
    c_init = c_msk

    if PHASES == "min":
        P.wait("sync", "ld", P.counts["ld"])
        P.op("sync", lambda e, S: e.dma_start(out=outD, in_=x1.rearrange("p c t -> p (c t)")),
             inc="st", inc_by=16)
        P.wait("sync", "st", P.counts["st"])
        return _finish(nc, P, sems_names=("ld", "st", "t", "v", "s"))

    nqtr_run = {"q1": 1, "q2": 2}.get(PHASES, NQTR)
    for qtr in range(nqtr_run):
        w0 = qtr * QTR
        c_x0 = dma(x0[:, :].rearrange("p (c t) -> p c t", c=NCH)[:, :, w0:w0 + 768],
                   x0q, key="x0q")

        for quad in range(int(PHASES[1]) if PHASES.startswith("V") else 2):
            c_wv = dma(wqD[:, WQ_MAP[("v", quad)]:WQ_MAP[("v", quad)] + 4096],
                       wv_b.rearrange("p c n -> p (c n)"), key="wv")
            P.wait("tensor", "ld", c_wv)
            P.wait("tensor", "ld", c_x0)
            for j in range(6):
                P.acquire("tensor", "pb0")
                for i in range(NCH):
                    t = mm(pb[0][:, :], x0q[:, i, j * 128:(j + 1) * 128],
                           wv_b[:, i, :], start=(i == 0), stop=(i == NCH - 1),
                           inc=(i == NCH - 1))
                P.consumed("wv", "t", t)
                P.acquire("vector", f"v_q{j}")
                P.wait("vector", "t", t)
                cv = P.op("vector", lambda e, S, j=j, quad=quad:
                          e.tensor_tensor(out=v_q[:, j, :], in0=pb[0][:, :],
                                          in1=bv[:, quad * 512:(quad + 1) * 512],
                                          op=OP.add), inc="v")
                P.consumed("pb0", "v", cv)

            for pq in range(0 if PHASES.startswith("V") else 4):
                p = quad * 4 + pq
                par_b = p % 2
                off = WQ_MAP[("qk", p)]
                c_qk = dma(wqD[:, off:off + 2048],
                           wqk_b[:, par_b, :, :].rearrange("p s n -> p (s n)"),
                           key=f"wqk{par_b}")
                P.wait("tensor", "ld", c_qk)
                P.acquire("tensor", "pb1")
                for i in range(NCH):
                    t = mm(pb[1][:, :], wqk_b[:, par_b, i, :],
                           x0q[:, i, 128:640], start=(i == 0), stop=(i == NCH - 1),
                           inc=(i == NCH - 1))
                P.acquire("scalar", "q_p")
                P.wait("scalar", "t", t)
                cq = P.op("scalar", lambda e, S, p=p:
                          e.activation(out=q_p[:, :], in_=pb[1][:, :], func=AF.Identity,
                                       bias=bqk[:, 2 * p:2 * p + 1], scale=1.0),
                          inc="s")
                P.consumed("pb1", "s", cq)
                P.acquire("tensor", "pb1")
                for i in range(NCH):
                    t = mm(pb[1][:, :], wqk_b[:, par_b, 8 + i, :],
                           x0q[:, i, 0:512], start=(i == 0), stop=(i == NCH - 1),
                           inc=(i == NCH - 1))
                P.acquire("scalar", "k_p")
                P.wait("scalar", "t", t)
                ck = P.op("scalar", lambda e, S, p=p:
                          e.activation(out=k_p[:, 0:512], in_=pb[1][:, :], func=AF.Identity,
                                       bias=bqk[:, 2 * p + 1:2 * p + 2], scale=1.0),
                          inc="s")
                P.consumed("pb1", "s", ck)
                P.acquire("tensor", "pb2")
                for i in range(NCH):
                    t = mm(pb[2][:, 0:256], wqk_b[:, par_b, 8 + i, :],
                           x0q[:, i, 512:768], start=(i == 0), stop=(i == NCH - 1),
                           inc=(i == NCH - 1))
                P.consumed(f"wqk{par_b}", "t", t)
                P.wait("scalar", "t", t)
                ck2 = P.op("scalar", lambda e, S, p=p:
                           e.activation(out=k_p[:, 512:768], in_=pb[2][:, 0:256],
                                        func=AF.Identity,
                                        bias=bqk[:, 2 * p + 1:2 * p + 2], scale=1.0),
                           inc="s")
                P.consumed("pb2", "s", ck2)

                units_on = not PHASES.startswith("QK")
                ustage = int(PHASES[1]) if PHASES.startswith("U") else 99
                ust = {}

                def emit_ST(i, p=p, qtr=qtr):
                    g2, par = i // 2, i % 2
                    rows = slice(par * 64, par * 64 + 64)
                    q = i % 2
                    bA, bB = (pb[3], pb[4]) if q == 0 else (pb[6], pb[7])
                    P.acquire("tensor", f"pb{3 if q == 0 else 6}")
                    P.acquire("tensor", f"pb{4 if q == 0 else 7}")
                    t_last = None
                    for j in range(4):
                        bank = bA if j < 2 else bB
                        seg = bank[:, (j % 2) * 256:(j % 2) * 256 + 256]
                        lt = 2 * g2 + j
                        gt = 4 * qtr + 2 * g2 + j
                        edge = (gt == 0) or (gt == 17)
                        r = mm(seg, k_p[rows, lt * 128:(lt + 1) * 128],
                               q_p[rows, g2 * 256:g2 * 256 + 256],
                               start=True, stop=not edge,
                               inc=(j == 3 and not edge))
                        if edge:
                            mc = 0 if gt == 0 else 128
                            r = mm(seg, msk[0:1, mc:mc + 128],
                                   cst[0:1, 0:256], start=False, stop=True,
                                   inc=(j == 3))
                        if j == 3:
                            t_last = r
                    ust[i] = t_last

                def emit_stats(i):
                    q = i % 2
                    c0 = 4 * q
                    bA, bB = (pb[3], pb[4]) if q == 0 else (pb[6], pb[7])
                    sb = pb[0] if q == 0 else pb[1]
                    P.wait("vector", "t", ust[i])
                    P.op("vector", lambda e, S, bA=bA, c0=c0:
                         e.reduce_max(out=neg_s[:, c0:c0 + 1],
                                      in_=bA[:, :].rearrange("p (a b) -> p a b", a=2),
                                      axis=AX.XY))
                    cr = P.op("vector", lambda e, S, bB=bB, c0=c0:
                              e.reduce_max(out=neg_s[:, c0 + 1:c0 + 2],
                                           in_=bB[:, :].rearrange("p (a b) -> p a b", a=2),
                                           axis=AX.XY), inc="v")
                    P.consumed(f"pb{3 if q == 0 else 6}", "v", cr)
                    P.consumed(f"pb{4 if q == 0 else 7}", "v", cr)
                    P.wait("vector", "v", cr)
                    cm = P.op("vector", lambda e, S, c0=c0:
                              e.tensor_tensor(out=neg_s[:, c0 + 2:c0 + 3],
                                              in0=neg_s[:, c0:c0 + 1],
                                              in1=neg_s[:, c0 + 1:c0 + 2],
                                              op=OP.max), inc="v")
                    P.acquire("tensor", f"pb{0 if q == 0 else 1}")
                    P.wait("tensor", "v", cm)
                    tt1 = P.op("tensor", lambda e, S, sb=sb, c0=c0:
                               e.transpose(sb[0:1, 384:512], neg_s[:, c0 + 2:c0 + 3],
                                           ident), inc="t")
                    P.wait("vector", "t", tt1)
                    cm2 = P.op("vector", lambda e, S, sb=sb, c0=c0:
                               e.reduce_max(out=neg_s[0:1, c0 + 3:c0 + 4],
                                            in_=sb[0:1, 384:512],
                                            axis=AX.X, negate=True), inc="v")
                    P.wait("tensor", "v", cm2)
                    tt2 = P.op("tensor", lambda e, S, sb=sb, c0=c0:
                               e.matmul(sb[:, 383:384], cst[0:1, 0:128],
                                        neg_s[0:1, c0 + 3:c0 + 4],
                                        start=True, stop=True), inc="t")
                    P.wait("vector", "t", tt2)
                    cng = P.op("vector", lambda e, S, sb=sb, q=q:
                               e.tensor_scalar(out=neg_s[:, 8 + q:9 + q],
                                               in0=sb[:, 383:384],
                                               scalar1=SCALE, scalar2=None,
                                               op0=OP.mult), inc="v")
                    P.consumed(f"pb{0 if q == 0 else 1}", "v", cng)
                    ust[(i, "cng")] = cng

                def emit_exp(i):
                    q = i % 2
                    bA, bB = (pb[3], pb[4]) if q == 0 else (pb[6], pb[7])
                    P.acquire("scalar", f"P{q}")
                    P.wait("scalar", "v", ust[(i, "cng")])
                    cexp = None
                    for j in range(4):
                        bank = bA if j < 2 else bB
                        seg = bank[:, (j % 2) * 256:(j % 2) * 256 + 256]
                        cexp = P.op("scalar", lambda e, S, j=j, seg=seg, q=q:
                                    e.activation(out=P_bufs[q][:, j, :], in_=seg,
                                                 func=AF.Exp,
                                                 bias=neg_s[:, 8 + q:9 + q],
                                                 scale=SCALE), inc="s")
                    P.consumed(f"pb{3 if q == 0 else 6}", "s", cexp)
                    P.consumed(f"pb{4 if q == 0 else 7}", "s", cexp)
                    ust[(i, "exp")] = cexp

                def emit_lPV(i, p=p):
                    g2, par = i // 2, i % 2
                    q = i % 2
                    rows = slice(par * 64, par * 64 + 64)
                    tile = pb[5] if g2 == 0 else pb[2]
                    tkey = "pb5" if g2 == 0 else "pb2"
                    P.wait("tensor", "s", ust[(i, "exp")])
                    if par == 0:
                        P.acquire("tensor", tkey)
                    for j in range(4):
                        mm(tile[rows, 0:256], ones_col, P_bufs[q][:, j, :],
                           start=(j == 0), stop=(j == 3))
                    lt0 = 2 * g2
                    hq = p % 4
                    dvs = slice((2 * hq + par) * 64, (2 * hq + par) * 64 + 64)
                    t_pv = None
                    for j in range(4):
                        t_pv = mm(tile[rows, 256:512], v_q[:, lt0 + j, dvs],
                                  P_bufs[q][:, j, :], start=(j == 0), stop=(j == 3),
                                  inc=(j == 3))
                    P.consumed(f"P{q}", "t", t_pv)
                    for j in range(4):
                        P.consumed(f"v_q{lt0 + j}", "t", t_pv)
                    P.consumed("q_p", "t", t_pv)
                    P.consumed("k_p", "t", t_pv)
                    ust[(i, "pv")] = t_pv

                def emit_norm(g2, p=p):
                    tile = pb[5] if g2 == 0 else pb[2]
                    tkey = "pb5" if g2 == 0 else "pb2"
                    P.wait("vector", "t", ust[(2 * g2 + 1, "pv")])
                    cr2 = P.op("vector", lambda e, S, tile=tile:
                               e.reciprocal(out=linv[:, :], in_=tile[:, 0:256]),
                               inc="v")
                    P.wait("vector", "v", cr2)
                    cn = P.op("vector", lambda e, S, p=p, g2=g2, tile=tile:
                              e.tensor_tensor(out=att[:, p, g2 * 256:g2 * 256 + 256],
                                              in0=tile[:, 256:512], in1=linv[:, :],
                                              op=OP.mult), inc="v")
                    P.consumed(tkey, "v", cn)

                if units_on:
                    emit_ST(0)
                    if ustage > 1:
                        emit_ST(1)
                        for i in range(4):
                            emit_stats(i)
                            if ustage > 3:
                                emit_exp(i)
                                if i + 2 < 4:
                                    emit_ST(i + 2)
                                if ustage > 5:
                                    emit_lPV(i)
                                    if i % 2 == 1 and ustage > 6:
                                        emit_norm(i // 2)
        if PHASES.startswith(("V", "QK", "U")):
            break
        if DEBUG_DUMP:
            P.wait("sync", "v", P.counts["v"])
            P.op("sync", lambda e, S, qtr=qtr:
                 e.dma_start(out=dbgD.rearrange("p (c t) -> p c t", c=NCH)[:, :, qtr * QTR:(qtr + 1) * QTR],
                             in_=att.rearrange("p c t -> p (c t)")),
                 inc="st", inc_by=16)

        for o in range(NCH):
            par_b = o % 2
            c_wp = dma(wpD[:, o * 1024:(o + 1) * 1024],
                       wp_b[:, par_b, :, :].rearrange("p s n -> p (s n)"),
                       key=f"wp{par_b}")
            P.wait("tensor", "ld", c_wp)
            bank = pb[1 + par_b]
            P.acquire("tensor", f"pb{1 + par_b}")
            for i in range(NCH):
                t = mm(bank[:, :], wp_b[:, par_b, i, :], att[:, i, :],
                       start=(i == 0), stop=(i == NCH - 1), inc=(i == NCH - 1))
            P.consumed(f"wp{par_b}", "t", t)
            if o == NCH - 1:
                P.consumed("att_done", "t", t)
            P.wait("vector", "t", t)
            cz = P.op("vector", lambda e, S, o=o, bank=bank:
                      e.tensor_tensor(out=zv[:, o, :], in0=bank[:, :],
                                      in1=x0q[:, o, 128:640], op=OP.add), inc="v")
            P.consumed(f"pb{1 + par_b}", "v", cz)
            P.wait("scalar", "v", cz)
            cz2 = P.op("scalar", lambda e, S, o=o:
                       e.activation(out=zv[:, o, :], in_=zv[:, o, :], func=AF.Identity,
                                    bias=bp[:, o:o + 1], scale=1.0), inc="s")
        P.acquire("tensor", "pb0")
        P.acquire("tensor", "pb7")
        for o in range(NCH):
            P.acquire("scalar", "sq_s")
            csq = P.op("scalar", lambda e, S, o=o:
                       e.activation(out=sq_s[:, :], in_=zv[:, o, :], func=AF.Square),
                       inc="s")
            P.wait("tensor", "s", csq)
            mm(pb[0][0:1, :], cst[:, 0:1], zv[:, o, :], start=(o == 0),
               stop=(o == NCH - 1))
            t = mm(pb[7][0:1, :], cst[:, 0:1], sq_s[:, :], start=(o == 0),
                   stop=(o == NCH - 1), inc=True)
            P.consumed("sq_s", "t", t)
        P.wait("vector", "t", t)
        P.op("vector", lambda e, S:
             e.tensor_scalar(out=r_mu, in0=pb[0][0:1, :], scalar1=1.0 / D,
                             scalar2=None, op0=OP.mult))
        cv = P.op("vector", lambda e, S:
                  e.tensor_scalar(out=r_ms, in0=pb[7][0:1, :], scalar1=1.0 / D,
                                  scalar2=None, op0=OP.mult), inc="v")
        P.consumed("pb0", "v", cv)
        P.consumed("pb7", "v", cv)
        P.op("vector", lambda e, S:
             e.tensor_tensor(out=r_var, in0=r_mu, in1=r_mu, op=OP.mult))
        P.op("vector", lambda e, S:
             e.tensor_tensor(out=r_var, in0=r_ms, in1=r_var, op=OP.subtract))
        cvar = P.op("vector", lambda e, S:
                    e.tensor_scalar(out=r_var, in0=r_var, scalar1=EPS,
                                    scalar2=None, op0=OP.add), inc="v")
        P.wait("scalar", "v", cvar)
        csd = P.op("scalar", lambda e, S:
                   e.activation(out=r_var, in_=r_var, func=AF.Sqrt),
                   inc="s")
        rinv_row = arena[0:1, A_P:A_P + 512]
        arow_row = arena[0:1, A_P + 512:A_P + 1024]
        P.wait("vector", "s", csd)
        P.op("vector", lambda e, S, r=rinv_row: e.reciprocal(out=r, in_=r_var))
        crw = P.op("vector", lambda e, S, r=rinv_row, a=arow_row:
                   e.tensor_tensor(out=a, in0=r_mu, in1=r,
                                   op=OP.mult), inc="v")
        P.acquire("tensor", "pb1")
        P.acquire("tensor", "pb2")
        P.wait("tensor", "v", crw)
        mm(pb[1][:, :], cst[0:1, 0:128], rinv_row, start=True, stop=True)
        t = mm(pb[2][:, :], cst[0:1, 0:128], arow_row, start=True, stop=True,
               inc=True)
        P.consumed("P0", "v", crw)
        P.consumed("P0", "t", P.counts["t"])
        P.wait("vector", "t", t)
        for o in range(NCH):
            qsl = slice(qtr * QTR, (qtr + 1) * QTR)
            P.op("vector", lambda e, S, o=o:
                 e.tensor_tensor(out=sq_s[:, :], in0=zv[:, o, :], in1=pb[1][:, :],
                                 op=OP.mult))
            P.op("vector", lambda e, S, o=o:
                 e.tensor_tensor(out=xf_s[:, :], in0=sq_s[:, :], in1=pb[2][:, :],
                                 op=OP.subtract))
            P.op("vector", lambda e, S, o=o, qsl=qsl:
                 e.tensor_copy(out=x1h[:, o, qsl], in_=xf_s[:, :]))
            cx1 = P.op("vector", lambda e, S, o=o, qsl=qsl:
                       e.tensor_tensor(out=x1l[:, o, qsl], in0=xf_s[:, :],
                                       in1=x1h[:, o, qsl], op=OP.subtract), inc="v")
        P.consumed("pb1", "v", cx1)
        P.consumed("pb2", "v", cx1)
        P.consumed("sq_s", "v", cx1)
        P.consumed("x0q", "v", cx1)
        P.consumed("x0q", "t", P.counts["t"])
    if PHASES.startswith(("V", "QK", "U")):
        P.wait("sync", "t", P.counts["t"])
        P.op("sync", lambda e, S: e.dma_start(out=outD, in_=x1.rearrange("p c t -> p (c t)")),
             inc="st", inc_by=16)
        P.wait("sync", "st", P.counts["st"])
        return _finish(nc, P, sems_names=("ld", "st", "t", "v", "s"))
    c_x1_done = cx1

    if PHASES == "route_marker":
        pass
    r_sc_all = arena[:, F_H:F_H + 256]
    r_sel_all = arena[:, F_H + 256:F_H + 512]
    r_cmp = arena[:, F_H + 512:F_H + 768]
    r_accA = arena[:, F_H + 768:F_H + 1024]
    r_accB = arena[:, F_H + 1024:F_H + 1280]
    r_fin = arena[:, F_H + 1280:F_H + 1536]
    r_sum = arena[:, F_H + 1536:F_H + 1552]
    r_inv = arena[:, F_H + 1552:F_H + 1568]
    r_srs = arena[:, F_H + 1568:F_H + 1584]
    sel3 = r_sel_all.rearrange("p (a b) -> p a b", a=16)
    fin3 = r_fin.rearrange("p (a b) -> p a b", a=16)

    P.wait("vector", "ld", c_init)
    P.op("vector", lambda e, S: e.tensor_copy(out=gwh[:, :], in_=gw[:, :]))
    cgw = P.op("vector", lambda e, S:
               e.tensor_tensor(out=gwl[:, :], in0=gw[:, :], in1=gwh[:, :],
                               op=OP.subtract), inc="v")
    P.wait("tensor", "v", c_x1_done)
    P.wait("tensor", "v", cgw)
    P.acquire("tensor", "pb6")
    for tt in range(16):
        t0 = tt * 128
        xgw = ((x1h, gwh), (x1h, gwl), (x1l, gwh))
        for pi, (xa, ga) in enumerate(xgw):
            for i in range(NCH):
                t = mm(pb[6][:, tt * 16:(tt + 1) * 16], xa[:, i, t0:t0 + 128],
                       ga[:, i * 16:(i + 1) * 16],
                       start=(pi == 0 and i == 0),
                       stop=(pi == 2 and i == NCH - 1),
                       inc=(tt == 15 and pi == 2 and i == NCH - 1))
    P.wait("scalar", "t", t)
    P.wait("scalar", "ld", c_init)
    cs = P.op("scalar", lambda e, S:
              e.activation(out=r_sc_all, in_=pb[6][:, 0:256], func=AF.Sigmoid),
              inc="s")
    P.consumed("pb6", "s", cs)
    P.wait("vector", "s", cs)
    P.op("vector", lambda e, S:
         e.tensor_tensor(out=r_sel_all.rearrange("p (a b) -> p a b", a=16),
                         in0=r_sc_all.rearrange("p (a b) -> p a b", a=16),
                         in1=eb[:, :].unsqueeze(1).broadcast_to([128, 16, 16]),
                         op=OP.add))
    accs = (r_accA, r_accB)
    for j in range(NE):
        P.op("vector", lambda e, S, j=j:
             e.tensor_tensor(out=r_cmp.rearrange("p (a b) -> p a b", a=16),
                             in0=sel3,
                             in1=sel3[:, :, j:j + 1].broadcast_to([128, 16, 16]),
                             op=OP.is_lt))
        if j == 0:
            P.op("vector", lambda e, S: e.tensor_copy(out=r_accA, in_=r_cmp))
        else:
            P.op("vector", lambda e, S, o=accs[j % 2], i0=accs[(j + 1) % 2]:
                 e.tensor_tensor(out=o, in0=i0, in1=r_cmp, op=OP.add))
    r_rank = accs[(NE - 1) % 2]
    P.op("vector", lambda e, S, a=r_rank:
         e.tensor_scalar(out=r_cmp, in0=a, scalar1=7.5, scalar2=None, op0=OP.is_le))
    P.op("vector", lambda e, S:
         e.tensor_tensor(out=r_fin, in0=r_sc_all, in1=r_cmp, op=OP.mult))
    c1 = P.op("vector", lambda e, S:
              e.reduce_sum(out=r_sum.rearrange("p (a b) -> p a b", b=1),
                           in_=fin3, axis=AX.X), inc="v")
    P.wait("vector", "v", c1)
    c2 = P.op("vector", lambda e, S: e.reciprocal(out=r_inv, in_=r_sum), inc="v")
    P.wait("vector", "v", c2)
    c3 = P.op("vector", lambda e, S:
              e.tensor_scalar(out=r_srs, in0=r_inv, scalar1=RSCALE, scalar2=None,
                              op0=OP.mult), inc="v")
    P.wait("vector", "v", c3)
    cw = P.op("vector", lambda e, S:
              e.tensor_tensor(out=fin3, in0=fin3,
                              in1=r_srs.rearrange("p (a b) -> p a b", b=1)
                              .broadcast_to([128, 16, 16]), op=OP.mult), inc="v")
    for tt in range(16):
        P.acquire("tensor", "pb7")
        P.wait("tensor", "v", cw)
        t = P.op("tensor", lambda e, S, tt=tt:
                 e.transpose(pb[7][0:16, 0:128], fin3[:, tt, :], ident), inc="t")
        P.wait("vector", "t", t)
        cc = P.op("vector", lambda e, S, tt=tt:
                  e.tensor_copy(out=comb[0:16, tt * 128:(tt + 1) * 128],
                                in_=pb[7][0:16, 0:128]), inc="v")
        P.consumed("pb7", "v", cc)
    c_comb_done = cc

    if PHASES == "route":
        P.wait("sync", "v", c_comb_done)
        P.op("sync", lambda e, S: e.dma_start(out=outD, in_=x1.rearrange("p c t -> p (c t)")),
             inc="st", inc_by=16)
        if DEBUG_DUMP:
            P.op("sync", lambda e, S: e.dma_start(out=dbgD[0:16, 0:TPC], in_=comb),
                 inc="st", inc_by=16)
            for i, tsr in enumerate((r_sc, r_sel, r_acc, r_mk, r_mk, r_w, r_s)):
                P.op("sync", lambda e, S, i=i, tsr=tsr:
                     e.dma_start(out=dbgD[:, TPC + i * 16:TPC + i * 16 + tsr.shape[-1]],
                                 in_=tsr), inc="st", inc_by=16)
        P.wait("sync", "st", P.counts["st"])
        return _finish(nc, P, sems_names=("ld", "st", "t", "v", "s"))

    pending_wgu = [None]
    for hf in range(NHF):
        h0 = hf * HF
        for e_i in range(NEX):
            e_off = e_i * WFFN_E_COLS
            routed = e_i < NE
            if routed:
                P.wait("tensor", "v", c_comb_done)
                for tt in range(2):
                    P.acquire("tensor", "pb6")
                    t = P.op("tensor", lambda e, S, e_i=e_i, tt=tt, h0=h0:
                             e.matmul(pb[6][:, :], sel_c[0:16, e_i * 128:(e_i + 1) * 128],
                                      comb[0:16, h0 + tt * 512:h0 + (tt + 1) * 512],
                                      start=True, stop=True), inc="t")
                    P.acquire("vector", f"cb{tt}")
                    P.wait("vector", "t", t)
                    ccb = P.op("vector", lambda e, S, tt=tt:
                               e.tensor_copy(out=cb_b[:, tt, :], in_=pb[6][:, :]),
                               inc="v")
                    P.consumed("pb6", "v", ccb)
            def wgu_load_split(cc, e_off=e_off):
                par_b = cc % 2
                c_w = dma(wffnD[:, e_off + cc * 2048:e_off + (cc + 1) * 2048],
                          wgu_f.rearrange("p s n -> p (s n)"), key="wgu_f")
                P.acquire("scalar", f"wgu16_{par_b}")
                P.wait("scalar", "ld", c_w)
                cca = P.op("scalar", lambda e, S, par_b=par_b:
                           e.activation(out=wgu16[:, par_b, 0, :, :].rearrange("p s n -> p (s n)"),
                                        in_=wgu_f.rearrange("p s n -> p (s n)"),
                                        func=AF.Identity), inc="s")
                P.wait("vector", "s", cca)
                csp = P.op("vector", lambda e, S, par_b=par_b:
                           e.tensor_tensor(out=wgu16[:, par_b, 1, :, :].rearrange("p s n -> p (s n)"),
                                           in0=wgu_f.rearrange("p s n -> p (s n)"),
                                           in1=wgu16[:, par_b, 0, :, :].rearrange("p s n -> p (s n)"),
                                           op=OP.subtract), inc="v")
                P.consumed("wgu_f", "v", csp)
                return cca, csp

            spl = {0: pending_wgu[0] if pending_wgu[0] is not None
                   else wgu_load_split(0)}
            pending_wgu[0] = None
            for c in range(NCH):
                par_b = c % 2
                if c + 1 < NCH:
                    spl[c + 1] = wgu_load_split(c + 1)
                cca, csp = spl[c]
                P.wait("tensor", "s", cca)
                P.wait("tensor", "v", csp)
                for tt in range(2):
                    ts0 = h0 + tt * 512
                    par_p = (c * 2 + tt) % 2
                    gbank = pb[par_p]
                    ubank = pb[2 + par_p]
                    passes = ((0, x1h), (1, x1h), (0, x1l))
                    P.acquire("tensor", f"pb{par_p}")
                    for pi, (wl, xa) in enumerate(passes):
                        for i in range(NCH):
                            mm(gbank[:, :], wgu16[:, par_b, wl, i, :],
                               xa[:, i, ts0:ts0 + 512], start=(pi == 0 and i == 0),
                               stop=(pi == 2 and i == NCH - 1))
                    P.acquire("tensor", f"pb{2 + par_p}")
                    for pi, (wl, xa) in enumerate(passes):
                        for i in range(NCH):
                            t = mm(ubank[:, :], wgu16[:, par_b, wl, 8 + i, :],
                                   xa[:, i, ts0:ts0 + 512], start=(pi == 0 and i == 0),
                                   stop=(pi == 2 and i == NCH - 1),
                                   inc=(pi == 2 and i == NCH - 1))
                    P.consumed(f"wgu16_{par_b}", "t", t)
                    P.acquire("scalar", "s_s")
                    P.wait("scalar", "t", t)
                    csl = P.op("scalar", lambda e, S, gbank=gbank:
                               e.activation(out=s_s[:, :], in_=gbank[:, :],
                                            func=AF.Silu), inc="s")
                    P.consumed(f"pb{par_p}", "s", csl)
                    P.wait("vector", "t", t)
                    P.wait("vector", "s", csl)
                    hsl = slice(tt * 512, (tt + 1) * 512)
                    if routed:
                        P.op("vector", lambda e, S, ubank=ubank, tt=tt:
                             e.tensor_tensor(out=t_s[:, :], in0=ubank[:, :],
                                             in1=cb_b[:, tt, :], op=OP.mult))
                        cf = P.op("vector", lambda e, S:
                                  e.tensor_tensor(out=s_s[:, :],
                                                  in0=s_s[:, :], in1=t_s[:, :],
                                                  op=OP.mult), inc="v")
                    else:
                        cf = P.op("vector", lambda e, S, ubank=ubank:
                                  e.tensor_tensor(out=s_s[:, :],
                                                  in0=s_s[:, :], in1=ubank[:, :],
                                                  op=OP.mult), inc="v")
                    P.consumed(f"pb{2 + par_p}", "v", cf)
                    P.wait("scalar", "v", cf)
                    chh = P.op("scalar", lambda e, S, c=c, hsl=hsl:
                               e.activation(out=h16[:, 0, c, hsl], in_=s_s[:, :],
                                            func=AF.Identity), inc="s")
                    P.wait("vector", "s", chh)
                    ch = P.op("vector", lambda e, S, c=c, hsl=hsl:
                              e.tensor_tensor(out=h16[:, 1, c, hsl], in0=s_s[:, :],
                                              in1=h16[:, 0, c, hsl],
                                              op=OP.subtract), inc="v")
                    P.consumed("s_s", "v", ch)
            c_h_done = ch
            def wd_load_split(oo, e_off=e_off):
                par_b = oo % 2
                c_w = dma(wffnD[:, e_off + 16384 + oo * 1024:e_off + 16384 + (oo + 1) * 1024],
                          wd_f.rearrange("p s n -> p (s n)"), key="wd_f")
                P.acquire("scalar", f"wd16_{par_b}")
                P.wait("scalar", "ld", c_w)
                cca = P.op("scalar", lambda e, S, par_b=par_b:
                           e.activation(out=wd16[:, par_b, 0, :, :].rearrange("p s n -> p (s n)"),
                                        in_=wd_f.rearrange("p s n -> p (s n)"),
                                        func=AF.Identity), inc="s")
                P.wait("vector", "s", cca)
                csp = P.op("vector", lambda e, S, par_b=par_b:
                           e.tensor_tensor(out=wd16[:, par_b, 1, :, :].rearrange("p s n -> p (s n)"),
                                           in0=wd_f.rearrange("p s n -> p (s n)"),
                                           in1=wd16[:, par_b, 0, :, :].rearrange("p s n -> p (s n)"),
                                           op=OP.subtract), inc="v")
                P.consumed("wd_f", "v", csp)
                return cca, csp

            dspl = {0: wd_load_split(0)}
            nxt = (e_i + 1) % NEX
            if not (hf == NHF - 1 and e_i == NEX - 1):
                pending_wgu[0] = wgu_load_split(0, e_off=nxt * WFFN_E_COLS)
            for o in range(NCH):
                par_b = o % 2
                if o + 1 < NCH:
                    dspl[o + 1] = wd_load_split(o + 1)
                cca, csp = dspl[o]
                P.wait("tensor", "s", cca)
                P.wait("tensor", "v", csp)
                P.wait("tensor", "v", c_h_done)
                for tt in range(2):
                    par_p = (o * 2 + tt) % 2
                    ybank = pb[4 + par_p]
                    hsl = slice(tt * 512, (tt + 1) * 512)
                    P.acquire("tensor", f"pb{4 + par_p}")
                    passes = ((0, 0), (1, 0), (0, 1))
                    for pi, (wl, hl) in enumerate(passes):
                        for c in range(NCH):
                            t = mm(ybank[:, :], wd16[:, par_b, wl, c, :],
                                   h16[:, hl, c, hsl], start=(pi == 0 and c == 0),
                                   stop=(pi == 2 and c == NCH - 1),
                                   inc=(pi == 2 and c == NCH - 1))
                    P.consumed(f"wd16_{par_b}", "t", t)
                    if e_i == NEX - 1:
                        P.consumed("h16", "t", t)
                    P.wait("vector", "t", t)
                    if e_i == 0:
                        P.op("vector", lambda e, S, o=o, tt=tt, h0=h0, ybank=ybank:
                             e.tensor_tensor(out=y_b[:, o, tt * 512:(tt + 1) * 512],
                                             in0=ybank[:, :],
                                             in1=x1h[:, o, h0 + tt * 512:h0 + (tt + 1) * 512],
                                             op=OP.add))
                        cy = P.op("vector", lambda e, S, o=o, tt=tt, h0=h0:
                                  e.tensor_tensor(out=y_b[:, o, tt * 512:(tt + 1) * 512],
                                                  in0=y_b[:, o, tt * 512:(tt + 1) * 512],
                                                  in1=x1l[:, o, h0 + tt * 512:h0 + (tt + 1) * 512],
                                                  op=OP.add), inc="v")
                    else:
                        cy = P.op("vector", lambda e, S, o=o, tt=tt, ybank=ybank:
                                  e.tensor_tensor(out=y_b[:, o, tt * 512:(tt + 1) * 512],
                                                  in0=y_b[:, o, tt * 512:(tt + 1) * 512],
                                                  in1=ybank[:, :], op=OP.add), inc="v")
                    P.consumed(f"pb{4 + par_p}", "v", cy)
        for tt in range(2):
            tsl = slice(tt * 512, (tt + 1) * 512)
            P.acquire("tensor", "pb6")
            P.acquire("tensor", "pb7")
            P.wait("scalar", "v", P.counts["v"])
            for o in range(NCH):
                P.acquire("scalar", "fsq")
                csq = P.op("scalar", lambda e, S, o=o, tsl=tsl:
                           e.activation(out=fsq[:, :], in_=y_b[:, o, tsl],
                                        func=AF.Square), inc="s")
                P.wait("tensor", "s", csq)
                mm(pb[6][0:1, :], cst[:, 0:1], y_b[:, o, tsl], start=(o == 0),
                   stop=(o == NCH - 1))
                t = mm(pb[7][0:1, :], cst[:, 0:1], fsq[:, :], start=(o == 0),
                       stop=(o == NCH - 1), inc=True)
                P.consumed("fsq", "t", t)
            P.wait("vector", "t", t)
            P.op("vector", lambda e, S:
                 e.tensor_scalar(out=r_mu, in0=pb[6][0:1, :], scalar1=1.0 / D,
                                 scalar2=None, op0=OP.mult))
            cv = P.op("vector", lambda e, S:
                      e.tensor_scalar(out=r_ms, in0=pb[7][0:1, :],
                                      scalar1=1.0 / D, scalar2=None, op0=OP.mult),
                      inc="v")
            P.consumed("pb6", "v", cv)
            P.consumed("pb7", "v", cv)
            P.op("vector", lambda e, S:
                 e.tensor_tensor(out=r_var, in0=r_mu, in1=r_mu, op=OP.mult))
            P.op("vector", lambda e, S:
                 e.tensor_tensor(out=r_var, in0=r_ms, in1=r_var, op=OP.subtract))
            cvar = P.op("vector", lambda e, S:
                        e.tensor_scalar(out=r_var, in0=r_var, scalar1=EPS,
                                        scalar2=None, op0=OP.add), inc="v")
            P.wait("scalar", "v", cvar)
            csd = P.op("scalar", lambda e, S:
                       e.activation(out=r_var, in_=r_var, func=AF.Sqrt),
                       inc="s")
            rinv2 = arena[0:1, F_CB:F_CB + 512]
            arow2 = arena[0:1, F_CB + 512:F_CB + 1024]
            P.wait("vector", "s", csd)
            P.op("vector", lambda e, S, r=rinv2: e.reciprocal(out=r, in_=r_var))
            crw = P.op("vector", lambda e, S, r=rinv2, a=arow2:
                       e.tensor_tensor(out=a, in0=r_mu,
                                       in1=r, op=OP.mult), inc="v")
            P.acquire("tensor", "pb0")
            P.acquire("tensor", "pb1")
            P.wait("tensor", "v", crw)
            mm(pb[0][:, :], cst[0:1, 0:128], rinv2, start=True, stop=True)
            t = mm(pb[1][:, :], cst[0:1, 0:128], arow2, start=True, stop=True,
                   inc=True)
            P.wait("vector", "t", t)
            for o in range(NCH):
                P.op("vector", lambda e, S, o=o, tsl=tsl:
                     e.tensor_tensor(out=fsq[:, :], in0=y_b[:, o, tsl], in1=pb[0][:, :],
                                     op=OP.mult))
                cz2 = P.op("vector", lambda e, S, o=o, tsl=tsl:
                           e.tensor_tensor(out=y_b[:, o, tsl], in0=fsq[:, :],
                                           in1=pb[1][:, :], op=OP.subtract), inc="v")
            P.consumed("pb0", "v", cz2)
            P.consumed("pb1", "v", cz2)
            P.consumed("fsq", "v", cz2)
        P.wait("sync", "v", P.counts["v"])
        c_st = P.op("sync", lambda e, S, hf=hf:
                    e.dma_start(out=outD.rearrange("p (c t) -> p c t", c=NCH)[:, :, hf * HF:(hf + 1) * HF],
                                in_=y_b.rearrange("p c t -> p (c t)")),
                    inc="st", inc_by=16)
        P.consumed("y_half", "st", P.counts["st"])
        P.acquire("vector", "y_half")

    P.wait("sync", "st", P.counts["st"])
    return _finish(nc, P, sems_names=("ld", "st", "t", "v", "s"))


def _finish(nc, P, sems_names):
    sems = {}
    for name in sems_names:
        sems[name] = nc.alloc_semaphore(f"sem_{name}")
    with nc.Block() as blk:
        @blk.sync
        def _(sync):
            P.emit("sync", sync, sems)

        @blk.tensor
        def _(tensor):
            P.emit("tensor", tensor, sems)

        @blk.vector
        def _(vector):
            P.emit("vector", vector, sems)

        @blk.scalar
        def _(scalar):
            P.emit("scalar", scalar, sems)

    return nc


def _prep_shared(L0_attn_w_qkv, L0_attn_b_qkv, L0_attn_w_proj, L0_attn_b_proj,
                 L0_ffn_gate_w, L0_ffn_expert_bias, L0_ffn_w_gate, L0_ffn_w_up,
                 L0_ffn_w_down, L0_ffn_shared_gate_w, L0_ffn_shared_up_w,
                 L0_ffn_shared_down_w):
    f = np.float32
    wqkvT = np.ascontiguousarray(np.asarray(L0_attn_w_qkv, f).T)
    wq = np.empty((128, WQ_COLS), f)
    for quad in range(2):
        off = WQ_MAP[("v", quad)]
        vsl = wqkvT[:, 2048 + quad * 512:2048 + (quad + 1) * 512]
        for i in range(8):
            wq[:, off + i * 512:off + (i + 1) * 512] = vsl[i * 128:(i + 1) * 128, :]
        for pq in range(4):
            p = quad * 4 + pq
            off = WQ_MAP[("qk", p)]
            qsl = wqkvT[:, p * 128:(p + 1) * 128]
            ksl = wqkvT[:, 1024 + p * 128:1024 + (p + 1) * 128]
            for i in range(8):
                wq[:, off + i * 128:off + (i + 1) * 128] = qsl[i * 128:(i + 1) * 128, :]
                wq[:, off + 1024 + i * 128:off + 1024 + (i + 1) * 128] = \
                    ksl[i * 128:(i + 1) * 128, :]

    wpT = np.ascontiguousarray(np.asarray(L0_attn_w_proj, f).T)
    wp = np.empty((128, 8192), f)
    for o in range(8):
        for i in range(8):
            wp[:, o * 1024 + i * 128:o * 1024 + (i + 1) * 128] = \
                wpT[i * 128:(i + 1) * 128, o * 128:(o + 1) * 128]

    wffn = np.empty((128, WFFN_COLS), f)
    wg_all = np.asarray(L0_ffn_w_gate, f)
    wu_all = np.asarray(L0_ffn_w_up, f)
    wd_all = np.asarray(L0_ffn_w_down, f)
    for e in range(NEX):
        if e < NE:
            gT = wg_all[e].T
            uT = wu_all[e].T
            dT = wd_all[e].T
        else:
            gT = np.asarray(L0_ffn_shared_gate_w, f).T
            uT = np.asarray(L0_ffn_shared_up_w, f).T
            dT = np.asarray(L0_ffn_shared_down_w, f).T
        eo = e * WFFN_E_COLS
        for c in range(8):
            for i in range(8):
                wffn[:, eo + c * 2048 + i * 128:eo + c * 2048 + (i + 1) * 128] = \
                    gT[i * 128:(i + 1) * 128, c * 128:(c + 1) * 128]
                wffn[:, eo + c * 2048 + 1024 + i * 128:eo + c * 2048 + 1024 + (i + 1) * 128] = \
                    uT[i * 128:(i + 1) * 128, c * 128:(c + 1) * 128]
        do = eo + 16384
        for o in range(8):
            for c in range(8):
                wffn[:, do + o * 1024 + c * 128:do + o * 1024 + (c + 1) * 128] = \
                    dT[c * 128:(c + 1) * 128, o * 128:(o + 1) * 128]

    cstv = np.zeros((128, 2432), f)
    cstv[:, 0:256] = 1.0
    cstv[:, 256:384] = np.eye(128, dtype=f)
    for e in range(16):
        cstv[e, 384 + e * 128:384 + (e + 1) * 128] = 1.0

    gwT = np.asarray(L0_ffn_gate_w, f).T
    gwv = np.empty((128, 128), f)
    for c in range(8):
        gwv[:, c * 16:(c + 1) * 16] = gwT[c * 128:(c + 1) * 128, :]

    ebv = np.tile(np.asarray(L0_ffn_expert_bias, f)[None, :], (128, 1))

    bqkv = np.empty((128, 16), f)
    bq = np.asarray(L0_attn_b_qkv, f)
    for p in range(8):
        bqkv[:, 2 * p] = bq[p * 128:(p + 1) * 128]
        bqkv[:, 2 * p + 1] = bq[1024 + p * 128:1024 + (p + 1) * 128]
    bvv = np.tile(bq[2048:3072][None, :], (128, 1)).astype(f)

    bpv = np.empty((128, 8), f)
    bpr = np.asarray(L0_attn_b_proj, f)
    for o in range(8):
        bpv[:, o] = bpr[o * 128:(o + 1) * 128]

    return {"cst": cstv, "wq": wq, "wp": wp, "gw": gwv, "eb": ebv,
            "bqk": bqkv, "bv": bvv, "bp": bpv, "wffn": wffn}


def _core_inputs(x, c):
    f = np.float32
    b, half = c // 2, c % 2
    s0 = half * TPC
    pad = np.zeros((SEQ + 2 * EXT, D), f)
    pad[EXT:EXT + SEQ] = x[b]
    sl = pad[s0:s0 + HALO]
    x0 = np.ascontiguousarray(sl.T)
    x0 = x0.reshape(NCH, 128, HALO).transpose(1, 0, 2).reshape(128, NCH * HALO)
    pos0 = s0 - EXT + np.arange(128)
    pos17 = s0 - EXT + 17 * 128 + np.arange(128)
    mskv = np.concatenate([
        np.where((pos0 < 0) | (pos0 >= SEQ), NEGBIG, 0.0),
        np.where((pos17 < 0) | (pos17 >= SEQ), NEGBIG, 0.0)]).astype(f)[None, :]
    return {"x0": np.ascontiguousarray(x0), "msk": mskv}


def kernel(L0_attn_x, L0_attn_w_qkv, L0_attn_b_qkv, L0_attn_w_proj, L0_attn_b_proj,
           L0_ffn_gate_w, L0_ffn_expert_bias, L0_ffn_w_gate, L0_ffn_w_up,
           L0_ffn_w_down, L0_ffn_shared_gate_w, L0_ffn_shared_up_w,
           L0_ffn_shared_down_w):
    f = np.float32
    x = np.asarray(L0_attn_x, f)
    shared = _prep_shared(L0_attn_w_qkv, L0_attn_b_qkv, L0_attn_w_proj,
                          L0_attn_b_proj, L0_ffn_gate_w, L0_ffn_expert_bias,
                          L0_ffn_w_gate, L0_ffn_w_up, L0_ffn_w_down,
                          L0_ffn_shared_gate_w, L0_ffn_shared_up_w,
                          L0_ffn_shared_down_w)

    in_maps = [dict(shared, **_core_inputs(x, c)) for c in range(8)]

    nc = build_program()
    res = run_bass_kernel_spmd(nc, in_maps, core_ids=list(range(8)))

    out = np.empty((NB, SEQ, D), f)
    for c in range(8):
        b, half = c // 2, c % 2
        s0 = half * TPC
        yc = res.results[c]["y"].reshape(128, NCH, TPC).transpose(1, 0, 2)
        out[b, s0:s0 + TPC] = yc.reshape(D, TPC).T
    return out


# version 09
# speedup vs baseline: 1.0975x; 1.0837x over previous
import numpy as np
import concourse.bass as bass
import concourse.mybir as mybir
from concourse.bass_utils import run_bass_kernel_spmd
from concourse.alu_op_type import AluOpType as OP

F32 = mybir.dt.float32
AF = mybir.ActivationFunctionType
AX = mybir.AxisListType

D = 1024
NH = 16
HD = 64
WIN = 256
EXT = 128
NB = 4
SEQ = 4096
TPC = 2048
HALO = TPC + 2 * EXT
NCH = D // 128
NE = 16
NEX = 17
TOPK = 8
RSCALE = 2.826
EPS = 1e-5
NEGBIG = -1e9
SCALE = HD ** -0.5

NQTR = 4
QTR = 512
NHF = 2
HF = 1024

def wq_map():
    m = {}
    off = 0
    for quad in range(2):
        m[("v", quad)] = off
        off += 8 * 512
        for pq in range(4):
            m[("qk", quad * 4 + pq)] = off
            off += 16 * 128
    return m, off

WQ_MAP, WQ_COLS = wq_map()
WFFN_E_COLS = 16 * 1024 + 8 * 1024
WFFN_COLS = NEX * WFFN_E_COLS

A_X0Q = 0;        A_X0Q_N = 8 * 768
A_WV = 6144;      A_WV_N = 8 * 512
A_WQK = 10240;    A_WQK_N = 2 * 2048
A_WP = 14336;     A_WP_N = 2 * 1024
A_Q = 16384;      A_Q_N = 512
A_K = 16896;      A_K_N = 768
A_V = 17664;      A_V_N = 6 * 512
A_P = 20736;      A_P_N = 4 * 256
A_ATT = 21760;    A_ATT_N = 8 * 512
A_SQ = 25856;     A_SQ_N = 512
A_LNR = 26368;    A_LNR_N = 512
A_NEG = 26880;    A_NEG_N = 12
A_LINV = 26892;   A_LINV_N = 256
A_MSK = 27148;    A_MSK_N = 256
ATT_END = A_MSK + A_MSK_N

F_Y = 0;          F_Y_N = 8 * 1024
F_H = 8192;       F_H_N = 8 * 1024
F_WGU = 16384;    F_WGU_N = 2048
F_WGU16 = 18432;  F_WGU16_N = 4096
F_WD = 22528;     F_WD_N = 1024
F_WD16 = 23552;   F_WD16_N = 2048
F_CB = 25600;     F_CB_N = 2 * 512
F_S = 26624;      F_S_N = 512
F_T = 27136;      F_T_N = 512
F_SQ = 27648;     F_SQ_N = 512
F_LNR = 28160;    F_LNR_N = 512
FFN_END = F_LNR + F_LNR_N

R_MS = 28672
R_VAR = 29184
A_XF = 29696
A_P2 = 30208
ARENA_COLS = 31232

ENGINES = ("sync", "tensor", "vector", "scalar", "gpsimd")

DEBUG_DUMP = False
PHASES = "all"


class Prog:

    def __init__(self):
        self.ops = {e: [] for e in ENGINES}
        self.counts = {}
        self.guards = {}

    def sem(self, name):
        self.counts[name] = 0

    def op(self, eng, fn, inc=None, inc_by=1):
        val = None
        if inc is not None:
            self.counts[inc] += inc_by
            val = self.counts[inc]
        self.ops[eng].append(("op", fn, inc, inc_by))
        return val

    def wait(self, eng, sem, val):
        assert val <= self.counts[sem], (sem, val, self.counts[sem])
        if val > 0:
            self.ops[eng].append(("wait", sem, val))

    def consumed(self, key, sem, val):
        g = self.guards.setdefault(key, {})
        g[sem] = max(g.get(sem, 0), val)

    def acquire(self, eng, key):
        g = self.guards.pop(key, None)
        if g:
            for sem, val in g.items():
                self.wait(eng, sem, val)

    def emit(self, eng, proxy, sems):
        for item in self.ops[eng]:
            if item[0] == "wait":
                _, sem, val = item
                proxy.wait_ge(sems[sem], val)
            else:
                _, fn, inc, inc_by = item
                instr = fn(proxy, sems)
                if inc is not None:
                    instr.then_inc(sems[inc], inc_by)


def build_program():
    nc = bass.Bass("TRN2", target_bir_lowering=False, debug=False,
                   detect_race_conditions=False, dynamic_dma_scratch_size=4096)

    x0 = nc.dram_tensor("x0", [128, NCH * HALO], F32, kind="ExternalInput").ap()
    mskD = nc.dram_tensor("msk", [1, 256], F32, kind="ExternalInput").ap()
    cstD = nc.dram_tensor("cst", [128, 2432], F32, kind="ExternalInput").ap()
    wqD = nc.dram_tensor("wq", [128, WQ_COLS], F32, kind="ExternalInput").ap()
    wpD = nc.dram_tensor("wp", [128, 8192], F32, kind="ExternalInput").ap()
    gwD = nc.dram_tensor("gw", [128, 128], F32, kind="ExternalInput").ap()
    ebD = nc.dram_tensor("eb", [128, 16], F32, kind="ExternalInput").ap()
    bqkD = nc.dram_tensor("bqk", [128, 16], F32, kind="ExternalInput").ap()
    bvD = nc.dram_tensor("bv", [128, 1024], F32, kind="ExternalInput").ap()
    bpD = nc.dram_tensor("bp", [128, 8], F32, kind="ExternalInput").ap()
    wffnD = nc.dram_tensor("wffn", [128, WFFN_COLS], F32, kind="ExternalInput").ap()
    outD = nc.dram_tensor("y", [128, NCH * TPC], F32, kind="ExternalOutput").ap()
    if DEBUG_DUMP:
        dbgD = nc.dram_tensor("dbg", [128, NCH * TPC], F32, kind="ExternalOutput").ap()

    BF16 = mybir.dt.bfloat16
    arena = nc.alloc_sbuf_tensor("arena", [128, ARENA_COLS], F32).ap()
    x1h = nc.alloc_sbuf_tensor("x1h", [128, NCH, TPC], BF16).ap()
    x1l = nc.alloc_sbuf_tensor("x1l", [128, NCH, TPC], BF16).ap()
    gwh = nc.alloc_sbuf_tensor("gwh", [128, 128], BF16).ap()
    gwl = nc.alloc_sbuf_tensor("gwl", [128, 128], BF16).ap()
    cst = nc.alloc_sbuf_tensor("cst_sb", [128, 2432], F32).ap()
    gw = nc.alloc_sbuf_tensor("gw_sb", [128, 128], F32).ap()
    eb = nc.alloc_sbuf_tensor("eb_sb", [128, 16], F32).ap()
    bqk = nc.alloc_sbuf_tensor("bqk_sb", [128, 16], F32).ap()
    bv = nc.alloc_sbuf_tensor("bv_sb", [128, 1024], F32).ap()
    bp = nc.alloc_sbuf_tensor("bp_sb", [128, 8], F32).ap()
    comb = nc.alloc_sbuf_tensor("comb_sb", [16, TPC], F32).ap()
    r_sc = nc.alloc_sbuf_tensor("r_sc", [128, 16], F32).ap()
    r_sel = nc.alloc_sbuf_tensor("r_sel", [128, 16], F32).ap()
    r_acc = nc.alloc_sbuf_tensor("r_acc", [128, 16], F32).ap()
    r_thr = nc.alloc_sbuf_tensor("r_thr", [128, 1], F32).ap()
    r_mk = nc.alloc_sbuf_tensor("r_mk", [128, 16], F32).ap()
    r_w = nc.alloc_sbuf_tensor("r_w", [128, 16], F32).ap()
    r_s = nc.alloc_sbuf_tensor("r_s", [128, 2], F32).ap()

    ones_col = cst[:, 0:64]
    ident = cst[:, 256:384]
    sel_c = cst[:, 384:2432]

    x0q = arena[:, A_X0Q:A_X0Q + A_X0Q_N].rearrange("p (c t) -> p c t", c=NCH)
    zv = x0q[:, :, 0:512]
    wv_b = arena[:, A_WV:A_WV + A_WV_N].rearrange("p (c n) -> p c n", c=8)
    wqk_b = arena[:, A_WQK:A_WQK + A_WQK_N].rearrange("p (b s n) -> p b s n", b=2, s=16)
    wp_b = arena[:, A_WP:A_WP + A_WP_N].rearrange("p (b s n) -> p b s n", b=2, s=8)
    q_p = arena[:, A_Q:A_Q + A_Q_N]
    k_p = arena[:, A_K:A_K + A_K_N]
    v_q = arena[:, A_V:A_V + A_V_N].rearrange("p (j n) -> p j n", j=6)
    P_bufs = (arena[:, A_P:A_P + A_P_N].rearrange("p (j n) -> p j n", j=4),
              arena[:, A_P2:A_P2 + A_P_N].rearrange("p (j n) -> p j n", j=4))
    att = arena[:, A_ATT:A_ATT + A_ATT_N].rearrange("p (c t) -> p c t", c=NCH)
    sq_s = arena[:, A_SQ:A_SQ + A_SQ_N]
    lnr = arena[:, A_LNR:A_LNR + A_LNR_N]
    r_mu = arena[0:1, A_LNR:A_LNR + 512]
    r_ms = arena[0:1, R_MS:R_MS + 512]
    r_var = arena[0:1, R_VAR:R_VAR + 512]
    neg_s = arena[:, A_NEG:A_NEG + A_NEG_N]
    linv = arena[:, A_LINV:A_LINV + A_LINV_N]
    msk = arena[:, A_MSK:A_MSK + A_MSK_N]

    y_b = arena[:, F_Y:F_Y + F_Y_N].rearrange("p (c t) -> p c t", c=NCH)
    h16 = arena[:, F_H:F_H + F_H_N].bitcast(BF16) \
        .rearrange("p (l c t) -> p l c t", l=2, c=NCH)
    wgu_f = arena[:, F_WGU:F_WGU + F_WGU_N].rearrange("p (s n) -> p s n", s=16)
    wgu16 = arena[:, F_WGU16:F_WGU16 + F_WGU16_N].bitcast(BF16) \
        .rearrange("p (b l s n) -> p b l s n", b=2, l=2, s=16)
    wd_f = arena[:, F_WD:F_WD + F_WD_N].rearrange("p (s n) -> p s n", s=8)
    wd16 = arena[:, F_WD16:F_WD16 + F_WD16_N].bitcast(BF16) \
        .rearrange("p (b l s n) -> p b l s n", b=2, l=2, s=8)
    xf_s = arena[:, A_XF:A_XF + 512]
    cb_b = arena[:, F_CB:F_CB + F_CB_N].rearrange("p (b n) -> p b n", b=2)
    s_s = arena[:, F_S:F_S + F_S_N]
    t_s = arena[:, F_T:F_T + F_T_N]
    fsq = arena[:, F_SQ:F_SQ + F_SQ_N]
    flnr = arena[:, F_LNR:F_LNR + F_LNR_N]

    pb = [nc.alloc_psum_tensor(f"pb{i}", [128, 512], F32).ap() for i in range(8)]

    P = Prog()
    for s in ("ld", "st", "t", "v", "s"):
        P.sem(s)

    def dma(src_ap, dst_ap, key=None, sem="ld"):
        if key is not None:
            P.acquire("sync", key)
        return P.op("sync", lambda e, S, a=src_ap, b=dst_ap: e.dma_start(out=b, in_=a),
                    inc=sem, inc_by=16)

    def mm(out_ap, lhsT, rhs, start, stop, inc=None):
        return P.op("tensor",
                    lambda e, S, o=out_ap, l=lhsT, r=rhs, st=start, sp=stop:
                    e.matmul(o, l, r, start=st, stop=sp),
                    inc=("t" if inc else None))

    dma(cstD, cst)
    dma(gwD, gw)
    dma(ebD, eb)
    dma(bqkD, bqk)
    dma(bvD, bv)
    dma(bpD, bp)
    c_msk = dma(mskD, msk[0:1, :])
    c_init = c_msk

    if PHASES == "min":
        P.wait("sync", "ld", P.counts["ld"])
        P.op("sync", lambda e, S: e.dma_start(out=outD, in_=x1.rearrange("p c t -> p (c t)")),
             inc="st", inc_by=16)
        P.wait("sync", "st", P.counts["st"])
        return _finish(nc, P, sems_names=("ld", "st", "t", "v", "s"))

    nqtr_run = {"q1": 1, "q2": 2}.get(PHASES, NQTR)
    for qtr in range(nqtr_run):
        w0 = qtr * QTR
        c_x0 = dma(x0[:, :].rearrange("p (c t) -> p c t", c=NCH)[:, :, w0:w0 + 768],
                   x0q, key="x0q")
        qk_t = {}

        for quad in range(int(PHASES[1]) if PHASES.startswith("V") else 2):
            c_wv = dma(wqD[:, WQ_MAP[("v", quad)]:WQ_MAP[("v", quad)] + 4096],
                       wv_b.rearrange("p c n -> p (c n)"), key="wv")
            P.wait("tensor", "ld", c_wv)
            P.wait("tensor", "ld", c_x0)
            for j in range(6):
                P.acquire("tensor", "pb0")
                for i in range(NCH):
                    t = mm(pb[0][:, :], x0q[:, i, j * 128:(j + 1) * 128],
                           wv_b[:, i, :], start=(i == 0), stop=(i == NCH - 1),
                           inc=(i == NCH - 1))
                P.consumed("wv", "t", t)
                P.acquire("vector", f"v_q{j}")
                P.wait("vector", "t", t)
                cv = P.op("vector", lambda e, S, j=j, quad=quad:
                          e.tensor_tensor(out=v_q[:, j, :], in0=pb[0][:, :],
                                          in1=bv[:, quad * 512:(quad + 1) * 512],
                                          op=OP.add), inc="v")
                P.consumed("pb0", "v", cv)

            for pq in range(0 if PHASES.startswith("V") else 4):
                p = quad * 4 + pq

                def emit_q_proj(pp):
                    par_b = pp % 2
                    off = WQ_MAP[("qk", pp)]
                    c_qk = dma(wqD[:, off:off + 2048],
                               wqk_b[:, par_b, :, :].rearrange("p s n -> p (s n)"),
                               key=f"wqk{par_b}")
                    P.wait("tensor", "ld", c_qk)
                    P.acquire("tensor", "pb1")
                    for i in range(NCH):
                        t = mm(pb[1][:, :], wqk_b[:, par_b, i, :],
                               x0q[:, i, 128:640], start=(i == 0),
                               stop=(i == NCH - 1), inc=(i == NCH - 1))
                    qk_t[(pp, "q")] = t

                def emit_k_proj(pp):
                    par_b = pp % 2
                    P.acquire("tensor", "pb2")
                    for i in range(NCH):
                        t = mm(pb[2][:, :], wqk_b[:, par_b, 8 + i, :],
                               x0q[:, i, 0:512], start=(i == 0),
                               stop=(i == NCH - 1), inc=(i == NCH - 1))
                    qk_t[(pp, "k")] = t

                if (p, "q") not in qk_t:
                    emit_q_proj(p)
                    emit_k_proj(p)

                par_b = p % 2
                P.acquire("scalar", "q_p")
                P.wait("scalar", "t", qk_t[(p, "q")])
                cq = P.op("scalar", lambda e, S, p=p:
                          e.activation(out=q_p[:, :], in_=pb[1][:, :], func=AF.Identity,
                                       bias=bqk[:, 2 * p:2 * p + 1], scale=1.0),
                          inc="s")
                P.consumed("pb1", "s", cq)
                P.acquire("scalar", "k_p")
                P.wait("scalar", "t", qk_t[(p, "k")])
                ck = P.op("scalar", lambda e, S, p=p:
                          e.activation(out=k_p[:, 0:512], in_=pb[2][:, :], func=AF.Identity,
                                       bias=bqk[:, 2 * p + 1:2 * p + 2], scale=1.0),
                          inc="s")
                P.consumed("pb2", "s", ck)
                P.acquire("tensor", "pb2")
                for i in range(NCH):
                    t = mm(pb[2][:, 0:256], wqk_b[:, par_b, 8 + i, :],
                           x0q[:, i, 512:768], start=(i == 0), stop=(i == NCH - 1),
                           inc=(i == NCH - 1))
                P.consumed(f"wqk{par_b}", "t", t)
                P.wait("scalar", "t", t)
                ck2 = P.op("scalar", lambda e, S, p=p:
                           e.activation(out=k_p[:, 512:768], in_=pb[2][:, 0:256],
                                        func=AF.Identity,
                                        bias=bqk[:, 2 * p + 1:2 * p + 2], scale=1.0),
                           inc="s")
                P.consumed("pb2", "s", ck2)

                units_on = not PHASES.startswith("QK")
                ust = {}

                def emit_ST(i, p=p, qtr=qtr):
                    g2, par = i // 2, i % 2
                    rows = slice(par * 64, par * 64 + 64)
                    q = i % 2
                    bA, bB = (pb[3], pb[4]) if q == 0 else (pb[6], pb[7])
                    P.acquire("tensor", f"pb{3 if q == 0 else 6}")
                    P.acquire("tensor", f"pb{4 if q == 0 else 7}")
                    t_last = None
                    for j in range(4):
                        bank = bA if j < 2 else bB
                        seg = bank[:, (j % 2) * 256:(j % 2) * 256 + 256]
                        lt = 2 * g2 + j
                        gt = 4 * qtr + 2 * g2 + j
                        edge = (gt == 0) or (gt == 17)
                        r = mm(seg, k_p[rows, lt * 128:(lt + 1) * 128],
                               q_p[rows, g2 * 256:g2 * 256 + 256],
                               start=True, stop=not edge,
                               inc=(j == 3 and not edge))
                        if edge:
                            mc = 0 if gt == 0 else 128
                            r = mm(seg, msk[0:1, mc:mc + 128],
                                   cst[0:1, 0:256], start=False, stop=True,
                                   inc=(j == 3))
                        if j == 3:
                            t_last = r
                    ust[i] = t_last

                def emit_stats(i):
                    q = i % 2
                    c0 = 4 * q
                    bA, bB = (pb[3], pb[4]) if q == 0 else (pb[6], pb[7])
                    sb = pb[0]
                    P.wait("vector", "t", ust[i])
                    P.op("vector", lambda e, S, bA=bA, c0=c0:
                         e.reduce_max(out=neg_s[:, c0:c0 + 1],
                                      in_=bA[:, :].rearrange("p (a b) -> p a b", a=2),
                                      axis=AX.XY))
                    cr = P.op("vector", lambda e, S, bB=bB, c0=c0:
                              e.reduce_max(out=neg_s[:, c0 + 1:c0 + 2],
                                           in_=bB[:, :].rearrange("p (a b) -> p a b", a=2),
                                           axis=AX.XY), inc="v")
                    P.consumed(f"pb{3 if q == 0 else 6}", "v", cr)
                    P.consumed(f"pb{4 if q == 0 else 7}", "v", cr)
                    P.wait("vector", "v", cr)
                    cm = P.op("vector", lambda e, S, c0=c0:
                              e.tensor_tensor(out=neg_s[:, c0 + 2:c0 + 3],
                                              in0=neg_s[:, c0:c0 + 1],
                                              in1=neg_s[:, c0 + 1:c0 + 2],
                                              op=OP.max), inc="v")
                    P.acquire("tensor", "pb0")
                    P.wait("tensor", "v", cm)
                    tt1 = P.op("tensor", lambda e, S, sb=sb, c0=c0:
                               e.transpose(sb[0:1, 384:512], neg_s[:, c0 + 2:c0 + 3],
                                           ident), inc="t")
                    P.wait("vector", "t", tt1)
                    cm2 = P.op("vector", lambda e, S, sb=sb, c0=c0:
                               e.reduce_max(out=neg_s[0:1, c0 + 3:c0 + 4],
                                            in_=sb[0:1, 384:512],
                                            axis=AX.X, negate=True), inc="v")
                    P.wait("tensor", "v", cm2)
                    tt2 = P.op("tensor", lambda e, S, sb=sb, c0=c0:
                               e.matmul(sb[:, 383:384], cst[0:1, 0:128],
                                        neg_s[0:1, c0 + 3:c0 + 4],
                                        start=True, stop=True), inc="t")
                    P.wait("vector", "t", tt2)
                    cng = P.op("vector", lambda e, S, sb=sb, q=q:
                               e.tensor_scalar(out=neg_s[:, 8 + q:9 + q],
                                               in0=sb[:, 383:384],
                                               scalar1=SCALE, scalar2=None,
                                               op0=OP.mult), inc="v")
                    P.consumed("pb0", "v", cng)
                    ust[(i, "cng")] = cng

                def emit_exp(i):
                    q = i % 2
                    bA, bB = (pb[3], pb[4]) if q == 0 else (pb[6], pb[7])
                    P.acquire("scalar", f"P{q}")
                    P.wait("scalar", "v", ust[(i, "cng")])
                    cexp = None
                    for j in range(4):
                        bank = bA if j < 2 else bB
                        seg = bank[:, (j % 2) * 256:(j % 2) * 256 + 256]
                        cexp = P.op("scalar", lambda e, S, j=j, seg=seg, q=q:
                                    e.activation(out=P_bufs[q][:, j, :], in_=seg,
                                                 func=AF.Exp,
                                                 bias=neg_s[:, 8 + q:9 + q],
                                                 scale=SCALE), inc="s")
                    P.consumed(f"pb{3 if q == 0 else 6}", "s", cexp)
                    P.consumed(f"pb{4 if q == 0 else 7}", "s", cexp)
                    ust[(i, "exp")] = cexp

                def emit_lPV(i, p=p):
                    g2, par = i // 2, i % 2
                    q = i % 2
                    rows = slice(par * 64, par * 64 + 64)
                    tile = pb[5] if g2 == 0 else pb[2]
                    tkey = "pb5" if g2 == 0 else "pb2"
                    P.wait("tensor", "s", ust[(i, "exp")])
                    if par == 0:
                        P.acquire("tensor", tkey)
                    for j in range(4):
                        mm(tile[rows, 0:256], ones_col, P_bufs[q][:, j, :],
                           start=(j == 0), stop=(j == 3))
                    lt0 = 2 * g2
                    hq = p % 4
                    dvs = slice((2 * hq + par) * 64, (2 * hq + par) * 64 + 64)
                    t_pv = None
                    for j in range(4):
                        t_pv = mm(tile[rows, 256:512], v_q[:, lt0 + j, dvs],
                                  P_bufs[q][:, j, :], start=(j == 0), stop=(j == 3),
                                  inc=(j == 3))
                    P.consumed(f"P{q}", "t", t_pv)
                    for j in range(4):
                        P.consumed(f"v_q{lt0 + j}", "t", t_pv)
                    P.consumed("q_p", "t", t_pv)
                    P.consumed("k_p", "t", t_pv)
                    ust[(i, "pv")] = t_pv

                def emit_norm(g2, p=p):
                    tile = pb[5] if g2 == 0 else pb[2]
                    tkey = "pb5" if g2 == 0 else "pb2"
                    P.wait("vector", "t", ust[(2 * g2 + 1, "pv")])
                    cr2 = P.op("vector", lambda e, S, tile=tile:
                               e.reciprocal(out=linv[:, :], in_=tile[:, 0:256]),
                               inc="v")
                    P.wait("vector", "v", cr2)
                    cn = P.op("vector", lambda e, S, p=p, g2=g2, tile=tile:
                              e.tensor_tensor(out=att[:, p, g2 * 256:g2 * 256 + 256],
                                              in0=tile[:, 256:512], in1=linv[:, :],
                                              op=OP.mult), inc="v")
                    P.consumed(tkey, "v", cn)

                if units_on:
                    nxt = p + 1 if p < 7 else None
                    emit_ST(0)
                    emit_ST(1)
                    emit_stats(0)
                    emit_exp(0)
                    emit_ST(2)
                    emit_lPV(0)
                    emit_stats(1)
                    emit_exp(1)
                    emit_ST(3)
                    emit_lPV(1)
                    emit_norm(0)
                    if nxt is not None:
                        emit_q_proj(nxt)
                    emit_stats(2)
                    emit_exp(2)
                    emit_lPV(2)
                    emit_stats(3)
                    emit_exp(3)
                    emit_lPV(3)
                    emit_norm(1)
                    if nxt is not None:
                        emit_k_proj(nxt)
        if PHASES.startswith(("V", "QK", "U")):
            break
        if DEBUG_DUMP:
            P.wait("sync", "v", P.counts["v"])
            P.op("sync", lambda e, S, qtr=qtr:
                 e.dma_start(out=dbgD.rearrange("p (c t) -> p c t", c=NCH)[:, :, qtr * QTR:(qtr + 1) * QTR],
                             in_=att.rearrange("p c t -> p (c t)")),
                 inc="st", inc_by=16)

        for o in range(NCH):
            par_b = o % 2
            c_wp = dma(wpD[:, o * 1024:(o + 1) * 1024],
                       wp_b[:, par_b, :, :].rearrange("p s n -> p (s n)"),
                       key=f"wp{par_b}")
            P.wait("tensor", "ld", c_wp)
            bank = pb[1 + par_b]
            P.acquire("tensor", f"pb{1 + par_b}")
            for i in range(NCH):
                t = mm(bank[:, :], wp_b[:, par_b, i, :], att[:, i, :],
                       start=(i == 0), stop=(i == NCH - 1), inc=(i == NCH - 1))
            P.consumed(f"wp{par_b}", "t", t)
            if o == NCH - 1:
                P.consumed("att_done", "t", t)
            P.wait("vector", "t", t)
            cz = P.op("vector", lambda e, S, o=o, bank=bank:
                      e.tensor_tensor(out=zv[:, o, :], in0=bank[:, :],
                                      in1=x0q[:, o, 128:640], op=OP.add), inc="v")
            P.consumed(f"pb{1 + par_b}", "v", cz)
            P.wait("scalar", "v", cz)
            cz2 = P.op("scalar", lambda e, S, o=o:
                       e.activation(out=zv[:, o, :], in_=zv[:, o, :], func=AF.Identity,
                                    bias=bp[:, o:o + 1], scale=1.0), inc="s")
        P.acquire("tensor", "pb0")
        P.acquire("tensor", "pb7")
        for o in range(NCH):
            P.acquire("scalar", "sq_s")
            csq = P.op("scalar", lambda e, S, o=o:
                       e.activation(out=sq_s[:, :], in_=zv[:, o, :], func=AF.Square),
                       inc="s")
            P.wait("tensor", "s", csq)
            mm(pb[0][0:1, :], cst[:, 0:1], zv[:, o, :], start=(o == 0),
               stop=(o == NCH - 1))
            t = mm(pb[7][0:1, :], cst[:, 0:1], sq_s[:, :], start=(o == 0),
                   stop=(o == NCH - 1), inc=True)
            P.consumed("sq_s", "t", t)
        P.wait("vector", "t", t)
        P.op("vector", lambda e, S:
             e.tensor_scalar(out=r_mu, in0=pb[0][0:1, :], scalar1=1.0 / D,
                             scalar2=None, op0=OP.mult))
        cv = P.op("vector", lambda e, S:
                  e.tensor_scalar(out=r_ms, in0=pb[7][0:1, :], scalar1=1.0 / D,
                                  scalar2=None, op0=OP.mult), inc="v")
        P.consumed("pb0", "v", cv)
        P.consumed("pb7", "v", cv)
        P.op("vector", lambda e, S:
             e.tensor_tensor(out=r_var, in0=r_mu, in1=r_mu, op=OP.mult))
        P.op("vector", lambda e, S:
             e.tensor_tensor(out=r_var, in0=r_ms, in1=r_var, op=OP.subtract))
        cvar = P.op("vector", lambda e, S:
                    e.tensor_scalar(out=r_var, in0=r_var, scalar1=EPS,
                                    scalar2=None, op0=OP.add), inc="v")
        P.wait("scalar", "v", cvar)
        csd = P.op("scalar", lambda e, S:
                   e.activation(out=r_var, in_=r_var, func=AF.Sqrt),
                   inc="s")
        rinv_row = arena[0:1, A_P:A_P + 512]
        arow_row = arena[0:1, A_P + 512:A_P + 1024]
        P.wait("vector", "s", csd)
        P.op("vector", lambda e, S, r=rinv_row: e.reciprocal(out=r, in_=r_var))
        crw = P.op("vector", lambda e, S, r=rinv_row, a=arow_row:
                   e.tensor_tensor(out=a, in0=r_mu, in1=r,
                                   op=OP.mult), inc="v")
        P.acquire("tensor", "pb1")
        P.acquire("tensor", "pb2")
        P.wait("tensor", "v", crw)
        mm(pb[1][:, :], cst[0:1, 0:128], rinv_row, start=True, stop=True)
        t = mm(pb[2][:, :], cst[0:1, 0:128], arow_row, start=True, stop=True,
               inc=True)
        P.consumed("P0", "v", crw)
        P.consumed("P0", "t", P.counts["t"])
        P.wait("vector", "t", t)
        for o in range(NCH):
            qsl = slice(qtr * QTR, (qtr + 1) * QTR)
            P.op("vector", lambda e, S, o=o:
                 e.tensor_tensor(out=sq_s[:, :], in0=zv[:, o, :], in1=pb[1][:, :],
                                 op=OP.mult))
            P.op("vector", lambda e, S, o=o:
                 e.tensor_tensor(out=xf_s[:, :], in0=sq_s[:, :], in1=pb[2][:, :],
                                 op=OP.subtract))
            P.op("vector", lambda e, S, o=o, qsl=qsl:
                 e.tensor_copy(out=x1h[:, o, qsl], in_=xf_s[:, :]))
            cx1 = P.op("vector", lambda e, S, o=o, qsl=qsl:
                       e.tensor_tensor(out=x1l[:, o, qsl], in0=xf_s[:, :],
                                       in1=x1h[:, o, qsl], op=OP.subtract), inc="v")
        P.consumed("pb1", "v", cx1)
        P.consumed("pb2", "v", cx1)
        P.consumed("sq_s", "v", cx1)
        P.consumed("x0q", "v", cx1)
        P.consumed("x0q", "t", P.counts["t"])
    if PHASES.startswith(("V", "QK", "U")):
        P.wait("sync", "t", P.counts["t"])
        P.op("sync", lambda e, S: e.dma_start(out=outD, in_=x1.rearrange("p c t -> p (c t)")),
             inc="st", inc_by=16)
        P.wait("sync", "st", P.counts["st"])
        return _finish(nc, P, sems_names=("ld", "st", "t", "v", "s"))
    c_x1_done = cx1

    if PHASES == "route_marker":
        pass
    r_sc_all = arena[:, F_H:F_H + 256]
    r_sel_all = arena[:, F_H + 256:F_H + 512]
    r_cmp = arena[:, F_H + 512:F_H + 768]
    r_accA = arena[:, F_H + 768:F_H + 1024]
    r_accB = arena[:, F_H + 1024:F_H + 1280]
    r_fin = arena[:, F_H + 1280:F_H + 1536]
    r_sum = arena[:, F_H + 1536:F_H + 1552]
    r_inv = arena[:, F_H + 1552:F_H + 1568]
    r_srs = arena[:, F_H + 1568:F_H + 1584]
    sel3 = r_sel_all.rearrange("p (a b) -> p a b", a=16)
    fin3 = r_fin.rearrange("p (a b) -> p a b", a=16)

    P.wait("vector", "ld", c_init)
    P.op("vector", lambda e, S: e.tensor_copy(out=gwh[:, :], in_=gw[:, :]))
    cgw = P.op("vector", lambda e, S:
               e.tensor_tensor(out=gwl[:, :], in0=gw[:, :], in1=gwh[:, :],
                               op=OP.subtract), inc="v")
    P.wait("tensor", "v", c_x1_done)
    P.wait("tensor", "v", cgw)
    P.acquire("tensor", "pb6")
    for tt in range(16):
        t0 = tt * 128
        xgw = ((x1h, gwh), (x1h, gwl), (x1l, gwh))
        for pi, (xa, ga) in enumerate(xgw):
            for i in range(NCH):
                t = mm(pb[6][:, tt * 16:(tt + 1) * 16], xa[:, i, t0:t0 + 128],
                       ga[:, i * 16:(i + 1) * 16],
                       start=(pi == 0 and i == 0),
                       stop=(pi == 2 and i == NCH - 1),
                       inc=(tt == 15 and pi == 2 and i == NCH - 1))
    P.wait("scalar", "t", t)
    P.wait("scalar", "ld", c_init)
    cs = P.op("scalar", lambda e, S:
              e.activation(out=r_sc_all, in_=pb[6][:, 0:256], func=AF.Sigmoid),
              inc="s")
    P.consumed("pb6", "s", cs)
    P.wait("vector", "s", cs)
    P.op("vector", lambda e, S:
         e.tensor_tensor(out=r_sel_all.rearrange("p (a b) -> p a b", a=16),
                         in0=r_sc_all.rearrange("p (a b) -> p a b", a=16),
                         in1=eb[:, :].unsqueeze(1).broadcast_to([128, 16, 16]),
                         op=OP.add))
    accs = (r_accA, r_accB)
    for j in range(NE):
        P.op("vector", lambda e, S, j=j:
             e.tensor_tensor(out=r_cmp.rearrange("p (a b) -> p a b", a=16),
                             in0=sel3,
                             in1=sel3[:, :, j:j + 1].broadcast_to([128, 16, 16]),
                             op=OP.is_lt))
        if j == 0:
            P.op("vector", lambda e, S: e.tensor_copy(out=r_accA, in_=r_cmp))
        else:
            P.op("vector", lambda e, S, o=accs[j % 2], i0=accs[(j + 1) % 2]:
                 e.tensor_tensor(out=o, in0=i0, in1=r_cmp, op=OP.add))
    r_rank = accs[(NE - 1) % 2]
    P.op("vector", lambda e, S, a=r_rank:
         e.tensor_scalar(out=r_cmp, in0=a, scalar1=7.5, scalar2=None, op0=OP.is_le))
    P.op("vector", lambda e, S:
         e.tensor_tensor(out=r_fin, in0=r_sc_all, in1=r_cmp, op=OP.mult))
    c1 = P.op("vector", lambda e, S:
              e.reduce_sum(out=r_sum.rearrange("p (a b) -> p a b", b=1),
                           in_=fin3, axis=AX.X), inc="v")
    P.wait("vector", "v", c1)
    c2 = P.op("vector", lambda e, S: e.reciprocal(out=r_inv, in_=r_sum), inc="v")
    P.wait("vector", "v", c2)
    c3 = P.op("vector", lambda e, S:
              e.tensor_scalar(out=r_srs, in0=r_inv, scalar1=RSCALE, scalar2=None,
                              op0=OP.mult), inc="v")
    P.wait("vector", "v", c3)
    cw = P.op("vector", lambda e, S:
              e.tensor_tensor(out=fin3, in0=fin3,
                              in1=r_srs.rearrange("p (a b) -> p a b", b=1)
                              .broadcast_to([128, 16, 16]), op=OP.mult), inc="v")
    for tt in range(16):
        P.acquire("tensor", "pb7")
        P.wait("tensor", "v", cw)
        t = P.op("tensor", lambda e, S, tt=tt:
                 e.transpose(pb[7][0:16, 0:128], fin3[:, tt, :], ident), inc="t")
        P.wait("vector", "t", t)
        cc = P.op("vector", lambda e, S, tt=tt:
                  e.tensor_copy(out=comb[0:16, tt * 128:(tt + 1) * 128],
                                in_=pb[7][0:16, 0:128]), inc="v")
        P.consumed("pb7", "v", cc)
    c_comb_done = cc

    if PHASES == "route":
        P.wait("sync", "v", c_comb_done)
        P.op("sync", lambda e, S: e.dma_start(out=outD, in_=x1.rearrange("p c t -> p (c t)")),
             inc="st", inc_by=16)
        if DEBUG_DUMP:
            P.op("sync", lambda e, S: e.dma_start(out=dbgD[0:16, 0:TPC], in_=comb),
                 inc="st", inc_by=16)
            for i, tsr in enumerate((r_sc, r_sel, r_acc, r_mk, r_mk, r_w, r_s)):
                P.op("sync", lambda e, S, i=i, tsr=tsr:
                     e.dma_start(out=dbgD[:, TPC + i * 16:TPC + i * 16 + tsr.shape[-1]],
                                 in_=tsr), inc="st", inc_by=16)
        P.wait("sync", "st", P.counts["st"])
        return _finish(nc, P, sems_names=("ld", "st", "t", "v", "s"))

    pending_wgu = [None]
    for hf in range(NHF):
        h0 = hf * HF
        for e_i in range(NEX):
            e_off = e_i * WFFN_E_COLS
            routed = e_i < NE
            if routed:
                P.wait("tensor", "v", c_comb_done)
                for tt in range(2):
                    P.acquire("tensor", "pb6")
                    t = P.op("tensor", lambda e, S, e_i=e_i, tt=tt, h0=h0:
                             e.matmul(pb[6][:, :], sel_c[0:16, e_i * 128:(e_i + 1) * 128],
                                      comb[0:16, h0 + tt * 512:h0 + (tt + 1) * 512],
                                      start=True, stop=True), inc="t")
                    P.acquire("vector", f"cb{tt}")
                    P.wait("vector", "t", t)
                    ccb = P.op("vector", lambda e, S, tt=tt:
                               e.tensor_copy(out=cb_b[:, tt, :], in_=pb[6][:, :]),
                               inc="v")
                    P.consumed("pb6", "v", ccb)
            def wgu_load_split(cc, e_off=e_off):
                par_b = cc % 2
                c_w = dma(wffnD[:, e_off + cc * 2048:e_off + (cc + 1) * 2048],
                          wgu_f.rearrange("p s n -> p (s n)"), key="wgu_f")
                P.acquire("scalar", f"wgu16_{par_b}")
                P.wait("scalar", "ld", c_w)
                cca = P.op("scalar", lambda e, S, par_b=par_b:
                           e.activation(out=wgu16[:, par_b, 0, :, :].rearrange("p s n -> p (s n)"),
                                        in_=wgu_f.rearrange("p s n -> p (s n)"),
                                        func=AF.Identity), inc="s")
                P.wait("vector", "s", cca)
                csp = P.op("vector", lambda e, S, par_b=par_b:
                           e.tensor_tensor(out=wgu16[:, par_b, 1, :, :].rearrange("p s n -> p (s n)"),
                                           in0=wgu_f.rearrange("p s n -> p (s n)"),
                                           in1=wgu16[:, par_b, 0, :, :].rearrange("p s n -> p (s n)"),
                                           op=OP.subtract), inc="v")
                P.consumed("wgu_f", "v", csp)
                return cca, csp

            spl = {0: pending_wgu[0] if pending_wgu[0] is not None
                   else wgu_load_split(0)}
            pending_wgu[0] = None
            for c in range(NCH):
                par_b = c % 2
                if c + 1 < NCH:
                    spl[c + 1] = wgu_load_split(c + 1)
                cca, csp = spl[c]
                P.wait("tensor", "s", cca)
                P.wait("tensor", "v", csp)
                for tt in range(2):
                    ts0 = h0 + tt * 512
                    par_p = (c * 2 + tt) % 2
                    gbank = pb[par_p]
                    ubank = pb[2 + par_p]
                    passes = ((0, x1h), (1, x1h), (0, x1l))
                    P.acquire("tensor", f"pb{par_p}")
                    for pi, (wl, xa) in enumerate(passes):
                        for i in range(NCH):
                            mm(gbank[:, :], wgu16[:, par_b, wl, i, :],
                               xa[:, i, ts0:ts0 + 512], start=(pi == 0 and i == 0),
                               stop=(pi == 2 and i == NCH - 1))
                    P.acquire("tensor", f"pb{2 + par_p}")
                    for pi, (wl, xa) in enumerate(passes):
                        for i in range(NCH):
                            t = mm(ubank[:, :], wgu16[:, par_b, wl, 8 + i, :],
                                   xa[:, i, ts0:ts0 + 512], start=(pi == 0 and i == 0),
                                   stop=(pi == 2 and i == NCH - 1),
                                   inc=(pi == 2 and i == NCH - 1))
                    P.consumed(f"wgu16_{par_b}", "t", t)
                    P.acquire("scalar", "s_s")
                    P.wait("scalar", "t", t)
                    csl = P.op("scalar", lambda e, S, gbank=gbank:
                               e.activation(out=s_s[:, :], in_=gbank[:, :],
                                            func=AF.Silu), inc="s")
                    P.consumed(f"pb{par_p}", "s", csl)
                    P.wait("vector", "t", t)
                    P.wait("vector", "s", csl)
                    hsl = slice(tt * 512, (tt + 1) * 512)
                    if routed:
                        P.op("vector", lambda e, S, ubank=ubank, tt=tt:
                             e.tensor_tensor(out=t_s[:, :], in0=ubank[:, :],
                                             in1=cb_b[:, tt, :], op=OP.mult))
                        cf = P.op("vector", lambda e, S:
                                  e.tensor_tensor(out=s_s[:, :],
                                                  in0=s_s[:, :], in1=t_s[:, :],
                                                  op=OP.mult), inc="v")
                    else:
                        cf = P.op("vector", lambda e, S, ubank=ubank:
                                  e.tensor_tensor(out=s_s[:, :],
                                                  in0=s_s[:, :], in1=ubank[:, :],
                                                  op=OP.mult), inc="v")
                    P.consumed(f"pb{2 + par_p}", "v", cf)
                    P.wait("scalar", "v", cf)
                    chh = P.op("scalar", lambda e, S, c=c, hsl=hsl:
                               e.activation(out=h16[:, 0, c, hsl], in_=s_s[:, :],
                                            func=AF.Identity), inc="s")
                    P.wait("vector", "s", chh)
                    ch = P.op("vector", lambda e, S, c=c, hsl=hsl:
                              e.tensor_tensor(out=h16[:, 1, c, hsl], in0=s_s[:, :],
                                              in1=h16[:, 0, c, hsl],
                                              op=OP.subtract), inc="v")
                    P.consumed("s_s", "v", ch)
            c_h_done = ch
            def wd_load_split(oo, e_off=e_off):
                par_b = oo % 2
                c_w = dma(wffnD[:, e_off + 16384 + oo * 1024:e_off + 16384 + (oo + 1) * 1024],
                          wd_f.rearrange("p s n -> p (s n)"), key="wd_f")
                P.acquire("scalar", f"wd16_{par_b}")
                P.wait("scalar", "ld", c_w)
                cca = P.op("scalar", lambda e, S, par_b=par_b:
                           e.activation(out=wd16[:, par_b, 0, :, :].rearrange("p s n -> p (s n)"),
                                        in_=wd_f.rearrange("p s n -> p (s n)"),
                                        func=AF.Identity), inc="s")
                P.wait("vector", "s", cca)
                csp = P.op("vector", lambda e, S, par_b=par_b:
                           e.tensor_tensor(out=wd16[:, par_b, 1, :, :].rearrange("p s n -> p (s n)"),
                                           in0=wd_f.rearrange("p s n -> p (s n)"),
                                           in1=wd16[:, par_b, 0, :, :].rearrange("p s n -> p (s n)"),
                                           op=OP.subtract), inc="v")
                P.consumed("wd_f", "v", csp)
                return cca, csp

            dspl = {0: wd_load_split(0)}
            nxt = (e_i + 1) % NEX
            if not (hf == NHF - 1 and e_i == NEX - 1):
                pending_wgu[0] = wgu_load_split(0, e_off=nxt * WFFN_E_COLS)
            for o in range(NCH):
                par_b = o % 2
                if o + 1 < NCH:
                    dspl[o + 1] = wd_load_split(o + 1)
                cca, csp = dspl[o]
                P.wait("tensor", "s", cca)
                P.wait("tensor", "v", csp)
                P.wait("tensor", "v", c_h_done)
                for tt in range(2):
                    par_p = (o * 2 + tt) % 2
                    ybank = pb[4 + par_p]
                    hsl = slice(tt * 512, (tt + 1) * 512)
                    P.acquire("tensor", f"pb{4 + par_p}")
                    passes = ((0, 0), (1, 0), (0, 1))
                    for pi, (wl, hl) in enumerate(passes):
                        for c in range(NCH):
                            t = mm(ybank[:, :], wd16[:, par_b, wl, c, :],
                                   h16[:, hl, c, hsl], start=(pi == 0 and c == 0),
                                   stop=(pi == 2 and c == NCH - 1),
                                   inc=(pi == 2 and c == NCH - 1))
                    P.consumed(f"wd16_{par_b}", "t", t)
                    if e_i == NEX - 1:
                        P.consumed("h16", "t", t)
                    P.wait("vector", "t", t)
                    if e_i == 0:
                        P.op("vector", lambda e, S, o=o, tt=tt, h0=h0, ybank=ybank:
                             e.tensor_tensor(out=y_b[:, o, tt * 512:(tt + 1) * 512],
                                             in0=ybank[:, :],
                                             in1=x1h[:, o, h0 + tt * 512:h0 + (tt + 1) * 512],
                                             op=OP.add))
                        cy = P.op("vector", lambda e, S, o=o, tt=tt, h0=h0:
                                  e.tensor_tensor(out=y_b[:, o, tt * 512:(tt + 1) * 512],
                                                  in0=y_b[:, o, tt * 512:(tt + 1) * 512],
                                                  in1=x1l[:, o, h0 + tt * 512:h0 + (tt + 1) * 512],
                                                  op=OP.add), inc="v")
                    else:
                        cy = P.op("vector", lambda e, S, o=o, tt=tt, ybank=ybank:
                                  e.tensor_tensor(out=y_b[:, o, tt * 512:(tt + 1) * 512],
                                                  in0=y_b[:, o, tt * 512:(tt + 1) * 512],
                                                  in1=ybank[:, :], op=OP.add), inc="v")
                    P.consumed(f"pb{4 + par_p}", "v", cy)
        for tt in range(2):
            tsl = slice(tt * 512, (tt + 1) * 512)
            P.acquire("tensor", "pb6")
            P.acquire("tensor", "pb7")
            P.wait("scalar", "v", P.counts["v"])
            for o in range(NCH):
                P.acquire("scalar", "fsq")
                csq = P.op("scalar", lambda e, S, o=o, tsl=tsl:
                           e.activation(out=fsq[:, :], in_=y_b[:, o, tsl],
                                        func=AF.Square), inc="s")
                P.wait("tensor", "s", csq)
                mm(pb[6][0:1, :], cst[:, 0:1], y_b[:, o, tsl], start=(o == 0),
                   stop=(o == NCH - 1))
                t = mm(pb[7][0:1, :], cst[:, 0:1], fsq[:, :], start=(o == 0),
                       stop=(o == NCH - 1), inc=True)
                P.consumed("fsq", "t", t)
            P.wait("vector", "t", t)
            P.op("vector", lambda e, S:
                 e.tensor_scalar(out=r_mu, in0=pb[6][0:1, :], scalar1=1.0 / D,
                                 scalar2=None, op0=OP.mult))
            cv = P.op("vector", lambda e, S:
                      e.tensor_scalar(out=r_ms, in0=pb[7][0:1, :],
                                      scalar1=1.0 / D, scalar2=None, op0=OP.mult),
                      inc="v")
            P.consumed("pb6", "v", cv)
            P.consumed("pb7", "v", cv)
            P.op("vector", lambda e, S:
                 e.tensor_tensor(out=r_var, in0=r_mu, in1=r_mu, op=OP.mult))
            P.op("vector", lambda e, S:
                 e.tensor_tensor(out=r_var, in0=r_ms, in1=r_var, op=OP.subtract))
            cvar = P.op("vector", lambda e, S:
                        e.tensor_scalar(out=r_var, in0=r_var, scalar1=EPS,
                                        scalar2=None, op0=OP.add), inc="v")
            P.wait("scalar", "v", cvar)
            csd = P.op("scalar", lambda e, S:
                       e.activation(out=r_var, in_=r_var, func=AF.Sqrt),
                       inc="s")
            rinv2 = arena[0:1, F_CB:F_CB + 512]
            arow2 = arena[0:1, F_CB + 512:F_CB + 1024]
            P.wait("vector", "s", csd)
            P.op("vector", lambda e, S, r=rinv2: e.reciprocal(out=r, in_=r_var))
            crw = P.op("vector", lambda e, S, r=rinv2, a=arow2:
                       e.tensor_tensor(out=a, in0=r_mu,
                                       in1=r, op=OP.mult), inc="v")
            P.acquire("tensor", "pb0")
            P.acquire("tensor", "pb1")
            P.wait("tensor", "v", crw)
            mm(pb[0][:, :], cst[0:1, 0:128], rinv2, start=True, stop=True)
            t = mm(pb[1][:, :], cst[0:1, 0:128], arow2, start=True, stop=True,
                   inc=True)
            P.wait("vector", "t", t)
            for o in range(NCH):
                P.op("vector", lambda e, S, o=o, tsl=tsl:
                     e.tensor_tensor(out=fsq[:, :], in0=y_b[:, o, tsl], in1=pb[0][:, :],
                                     op=OP.mult))
                cz2 = P.op("vector", lambda e, S, o=o, tsl=tsl:
                           e.tensor_tensor(out=y_b[:, o, tsl], in0=fsq[:, :],
                                           in1=pb[1][:, :], op=OP.subtract), inc="v")
            P.consumed("pb0", "v", cz2)
            P.consumed("pb1", "v", cz2)
            P.consumed("fsq", "v", cz2)
        P.wait("sync", "v", P.counts["v"])
        c_st = P.op("sync", lambda e, S, hf=hf:
                    e.dma_start(out=outD.rearrange("p (c t) -> p c t", c=NCH)[:, :, hf * HF:(hf + 1) * HF],
                                in_=y_b.rearrange("p c t -> p (c t)")),
                    inc="st", inc_by=16)
        P.consumed("y_half", "st", P.counts["st"])
        P.acquire("vector", "y_half")

    P.wait("sync", "st", P.counts["st"])
    return _finish(nc, P, sems_names=("ld", "st", "t", "v", "s"))


def _finish(nc, P, sems_names):
    sems = {}
    for name in sems_names:
        sems[name] = nc.alloc_semaphore(f"sem_{name}")
    with nc.Block() as blk:
        @blk.sync
        def _(sync):
            P.emit("sync", sync, sems)

        @blk.tensor
        def _(tensor):
            P.emit("tensor", tensor, sems)

        @blk.vector
        def _(vector):
            P.emit("vector", vector, sems)

        @blk.scalar
        def _(scalar):
            P.emit("scalar", scalar, sems)

    return nc


def _prep_shared(L0_attn_w_qkv, L0_attn_b_qkv, L0_attn_w_proj, L0_attn_b_proj,
                 L0_ffn_gate_w, L0_ffn_expert_bias, L0_ffn_w_gate, L0_ffn_w_up,
                 L0_ffn_w_down, L0_ffn_shared_gate_w, L0_ffn_shared_up_w,
                 L0_ffn_shared_down_w):
    f = np.float32
    wqkvT = np.ascontiguousarray(np.asarray(L0_attn_w_qkv, f).T)
    wq = np.empty((128, WQ_COLS), f)
    for quad in range(2):
        off = WQ_MAP[("v", quad)]
        vsl = wqkvT[:, 2048 + quad * 512:2048 + (quad + 1) * 512]
        for i in range(8):
            wq[:, off + i * 512:off + (i + 1) * 512] = vsl[i * 128:(i + 1) * 128, :]
        for pq in range(4):
            p = quad * 4 + pq
            off = WQ_MAP[("qk", p)]
            qsl = wqkvT[:, p * 128:(p + 1) * 128]
            ksl = wqkvT[:, 1024 + p * 128:1024 + (p + 1) * 128]
            for i in range(8):
                wq[:, off + i * 128:off + (i + 1) * 128] = qsl[i * 128:(i + 1) * 128, :]
                wq[:, off + 1024 + i * 128:off + 1024 + (i + 1) * 128] = \
                    ksl[i * 128:(i + 1) * 128, :]

    wpT = np.ascontiguousarray(np.asarray(L0_attn_w_proj, f).T)
    wp = np.empty((128, 8192), f)
    for o in range(8):
        for i in range(8):
            wp[:, o * 1024 + i * 128:o * 1024 + (i + 1) * 128] = \
                wpT[i * 128:(i + 1) * 128, o * 128:(o + 1) * 128]

    wffn = np.empty((128, WFFN_COLS), f)
    wg_all = np.asarray(L0_ffn_w_gate, f)
    wu_all = np.asarray(L0_ffn_w_up, f)
    wd_all = np.asarray(L0_ffn_w_down, f)
    for e in range(NEX):
        if e < NE:
            gT = wg_all[e].T
            uT = wu_all[e].T
            dT = wd_all[e].T
        else:
            gT = np.asarray(L0_ffn_shared_gate_w, f).T
            uT = np.asarray(L0_ffn_shared_up_w, f).T
            dT = np.asarray(L0_ffn_shared_down_w, f).T
        eo = e * WFFN_E_COLS
        for c in range(8):
            for i in range(8):
                wffn[:, eo + c * 2048 + i * 128:eo + c * 2048 + (i + 1) * 128] = \
                    gT[i * 128:(i + 1) * 128, c * 128:(c + 1) * 128]
                wffn[:, eo + c * 2048 + 1024 + i * 128:eo + c * 2048 + 1024 + (i + 1) * 128] = \
                    uT[i * 128:(i + 1) * 128, c * 128:(c + 1) * 128]
        do = eo + 16384
        for o in range(8):
            for c in range(8):
                wffn[:, do + o * 1024 + c * 128:do + o * 1024 + (c + 1) * 128] = \
                    dT[c * 128:(c + 1) * 128, o * 128:(o + 1) * 128]

    cstv = np.zeros((128, 2432), f)
    cstv[:, 0:256] = 1.0
    cstv[:, 256:384] = np.eye(128, dtype=f)
    for e in range(16):
        cstv[e, 384 + e * 128:384 + (e + 1) * 128] = 1.0

    gwT = np.asarray(L0_ffn_gate_w, f).T
    gwv = np.empty((128, 128), f)
    for c in range(8):
        gwv[:, c * 16:(c + 1) * 16] = gwT[c * 128:(c + 1) * 128, :]

    ebv = np.tile(np.asarray(L0_ffn_expert_bias, f)[None, :], (128, 1))

    bqkv = np.empty((128, 16), f)
    bq = np.asarray(L0_attn_b_qkv, f)
    for p in range(8):
        bqkv[:, 2 * p] = bq[p * 128:(p + 1) * 128]
        bqkv[:, 2 * p + 1] = bq[1024 + p * 128:1024 + (p + 1) * 128]
    bvv = np.tile(bq[2048:3072][None, :], (128, 1)).astype(f)

    bpv = np.empty((128, 8), f)
    bpr = np.asarray(L0_attn_b_proj, f)
    for o in range(8):
        bpv[:, o] = bpr[o * 128:(o + 1) * 128]

    return {"cst": cstv, "wq": wq, "wp": wp, "gw": gwv, "eb": ebv,
            "bqk": bqkv, "bv": bvv, "bp": bpv, "wffn": wffn}


def _core_inputs(x, c):
    f = np.float32
    b, half = c // 2, c % 2
    s0 = half * TPC
    pad = np.zeros((SEQ + 2 * EXT, D), f)
    pad[EXT:EXT + SEQ] = x[b]
    sl = pad[s0:s0 + HALO]
    x0 = np.ascontiguousarray(sl.T)
    x0 = x0.reshape(NCH, 128, HALO).transpose(1, 0, 2).reshape(128, NCH * HALO)
    pos0 = s0 - EXT + np.arange(128)
    pos17 = s0 - EXT + 17 * 128 + np.arange(128)
    mskv = np.concatenate([
        np.where((pos0 < 0) | (pos0 >= SEQ), NEGBIG, 0.0),
        np.where((pos17 < 0) | (pos17 >= SEQ), NEGBIG, 0.0)]).astype(f)[None, :]
    return {"x0": np.ascontiguousarray(x0), "msk": mskv}


def kernel(L0_attn_x, L0_attn_w_qkv, L0_attn_b_qkv, L0_attn_w_proj, L0_attn_b_proj,
           L0_ffn_gate_w, L0_ffn_expert_bias, L0_ffn_w_gate, L0_ffn_w_up,
           L0_ffn_w_down, L0_ffn_shared_gate_w, L0_ffn_shared_up_w,
           L0_ffn_shared_down_w):
    f = np.float32
    x = np.asarray(L0_attn_x, f)
    shared = _prep_shared(L0_attn_w_qkv, L0_attn_b_qkv, L0_attn_w_proj,
                          L0_attn_b_proj, L0_ffn_gate_w, L0_ffn_expert_bias,
                          L0_ffn_w_gate, L0_ffn_w_up, L0_ffn_w_down,
                          L0_ffn_shared_gate_w, L0_ffn_shared_up_w,
                          L0_ffn_shared_down_w)

    in_maps = [dict(shared, **_core_inputs(x, c)) for c in range(8)]

    nc = build_program()
    res = run_bass_kernel_spmd(nc, in_maps, core_ids=list(range(8)))

    out = np.empty((NB, SEQ, D), f)
    for c in range(8):
        b, half = c // 2, c % 2
        s0 = half * TPC
        yc = res.results[c]["y"].reshape(128, NCH, TPC).transpose(1, 0, 2)
        out[b, s0:s0 + TPC] = yc.reshape(D, TPC).T
    return out
